# version 12 (final)
# speedup vs baseline: 1.0352x; 1.0006x over previous
import numpy as np
import ml_dtypes
import concourse.bass as bass
import concourse.mybir as mybir
from concourse.bass_utils import run_bass_kernel_spmd

F32 = mybir.dt.float32
BF16 = mybir.dt.bfloat16
AF = mybir.ActivationFunctionType
ALU = mybir.AluOpType
AX = mybir.AxisListType
BF = ml_dtypes.bfloat16

D = 1024
T = 4096
NCORES = 8
TILE = 512
NT = T // TILE

ENGS = ("sync", "gpsimd", "tensor", "vector", "scalar")



def lhst(W):
    K, M = W.shape
    assert K % 128 == 0 and M % 128 == 0, (K, M)
    return np.ascontiguousarray(
        W.reshape(K // 128, 128, M // 128, 128).transpose(1, 0, 2, 3).reshape(128, -1))


def lhst_small(W):
    K, M = W.shape
    assert K % 128 == 0 and M <= 128
    return np.ascontiguousarray(W.reshape(K // 128, 128, M).transpose(1, 0, 2).reshape(128, -1))


def col128(v):
    (Kd,) = v.shape
    return np.ascontiguousarray(v.reshape(Kd // 128, 128).T)



class Sem:
    def __init__(self, h, name, eng):
        self.h = h
        self.name = name
        self.eng = eng
        self.n = 0


class Res:
    __slots__ = ("w", "r")

    def __init__(self):
        self.w = None
        self.r = []


class Ctx:

    def __init__(self, nc, which, eng, sems, env):
        self.nc = nc
        self.which = which
        self.eng = eng
        self.s = sems
        self.env = env
        self.watermark = {}
        self.pending = []

    def emit(self, eng_name, fn, inc=None, inc_n=1):
        if self.which == eng_name:
            inline = self.flush_waits(keep_one=True)
            inst = fn(self.eng)
            if inline is not None:
                inst._wait_ge(inline[0].h, inline[1])
            if inc is not None:
                inst.then_inc(inc.h, inc_n)
        if inc is not None:
            inc.n += inc_n
        return inc.n if inc is not None else None

    def wait(self, eng_name, sem, val):
        if val is None or val <= 0:
            return
        if sem.eng == eng_name == "tensor":
            return
        key = (eng_name, sem.name)
        if self.watermark.get(key, -1) >= val:
            return
        self.watermark[key] = val
        if self.which == eng_name:
            self.pending.append((sem, val))

    def flush_waits(self, keep_one=False):
        pend = self.pending
        self.pending = []
        inline = pend.pop() if (keep_one and pend) else None
        for (sem, val) in pend:
            self.eng.wait_ge(sem.h, val)
        return inline

    def _v(self, sem, cnt):
        return sem.n if sem.eng is None else cnt

    def rd(self, eng_name, *ress):
        for res in ress:
            if res.w is not None:
                self.wait(eng_name, res.w[0], self._v(res.w[0], res.w[1]))

    def wr(self, eng_name, *ress):
        for res in ress:
            for (sem, cnt) in res.r:
                self.wait(eng_name, sem, self._v(sem, cnt))
            if res.w is not None:
                self.wait(eng_name, res.w[0], self._v(res.w[0], res.w[1]))

    def did_rd(self, sem, *ress):
        for res in ress:
            res.r.append((sem, sem.n))

    def did_wr(self, sem, *ress):
        for res in ress:
            res.w = (sem, sem.n)
            res.r = []

    def step(self, eng_name, fn, reads=(), writes=(), inc=None, inc_n=1):
        if inc is None:
            inc = self.s[{"sync": "ld", "gpsimd": "pool", "tensor": "mm",
                          "vector": "dve", "scalar": "act"}[eng_name]]
        self.rd(eng_name, *reads)
        self.wr(eng_name, *writes)
        self.emit(eng_name, fn, inc=inc, inc_n=inc_n)
        self.did_rd(inc, *reads)
        self.did_wr(inc, *writes)

    def dma(self, queue, out, in_, reads=(), writes=(), sem=None):
        if sem is None:
            sem = self.s["ld"] if queue == "sync" else self.s["w"]
        self.rd(queue, *reads)
        self.wr(queue, *writes)
        self.emit(queue, lambda e: e.dma_start(out=out, in_=in_), inc=sem, inc_n=16)
        self.did_rd(sem, *reads)
        self.did_wr(sem, *writes)

    def mm_group(self, items, bank_res, reads=(), n_end_inc=True):
        self.rd("tensor", *reads)
        self.wr("tensor", bank_res)
        last = len(items) - 1
        for i, (o, l, r) in enumerate(items):
            inc = self.s["mm"] if (i == last and n_end_inc) else None
            self.emit("tensor", lambda e, o=o, l=l, r=r, i=i: e.matmul(
                o, l, r, start=(i == 0), stop=(i == last)), inc=inc)
        self.did_rd(self.s["mm"], *reads)
        self.did_wr(self.s["mm"], bank_res)



def emit_gating(c: Ctx, env, t0, score_items, score_reads, topk, bias_ap=None,
                bias_res=None):
    s = c.s
    ps = env["ps"]
    A32 = env["A32"]
    AB = env["AB"]
    cbf = env["cbf"]
    cf32 = env["cf32"]
    cfo = env["cf32_off"]
    R = env["R"]

    def res(name):
        if name not in R:
            R[name] = Res()
        return R[name]

    def bank(i):
        return res(("bank", i))

    AB_CB = AB[:, 49152:53248]
    A_G = A32[:, 20992:22528]
    r_g = res("A_G")
    r_cb = res("AB_CB")

    c.mm_group(score_items, bank(0), reads=score_reads)
    sc = A_G[0:8, 0:512]
    if bias_ap is not None:
        c.step("vector", lambda e: e.tensor_scalar(sc, ps[0:8, 0, :], bias_ap,
                                                   None, ALU.add),
               reads=(bank(0), bias_res), writes=(r_g,))
    else:
        c.step("vector", lambda e: e.tensor_copy(sc, ps[0:8, 0, :]),
               reads=(bank(0),), writes=(r_g,))
    c.rd("tensor", r_g, env["r_consts"])
    c.wr("tensor", bank(1))
    for j in range(4):
        c.emit("tensor", lambda e, j=j: e.transpose(
            ps[:, 1, j * 8:(j + 1) * 8], sc[:, j * 128:(j + 1) * 128],
            cf32[0:8, cfo["id"]:cfo["id"] + 8]), inc=s["mm"] if j == 3 else None)
    c.did_rd(s["mm"], r_g)
    c.did_wr(s["mm"], bank(1))

    g = A_G[:, 512:544].rearrange("p (j e) -> p j e", j=4)
    gm = A_G[:, 544:548]
    ge = A_G[:, 548:580].rearrange("p (j e) -> p j e", j=4)
    gs = A_G[:, 580:584]
    pw = A_G[:, 584:616].rearrange("p (j e) -> p j e", j=4)
    mv = A_G[:, 616:620]
    msk = A_G[:, 620:652].rearrange("p (j e) -> p j e", j=4)
    c.step("vector", lambda e: e.tensor_copy(
        g[:, :, :], ps[:, 1, 0:32].rearrange("p (j e) -> p j e", j=4)),
        reads=(bank(1),), writes=(r_g,))
    c.step("vector", lambda e: e.tensor_reduce(gm[:, :], g[:, :, :], AX.X, ALU.max),
           reads=(r_g,), writes=(r_g,))
    c.step("vector", lambda e: e.tensor_tensor(
        ge[:, :, :], g[:, :, :], gm[:, :].unsqueeze(2).broadcast_to([128, 4, 8]),
        ALU.subtract), reads=(r_g,), writes=(r_g,))
    c.step("scalar", lambda e: e.activation(out=ge[:, :, :], in_=ge[:, :, :], func=AF.Exp),
           reads=(r_g,), writes=(r_g,))
    c.step("vector", lambda e: e.tensor_reduce(gs[:, :], ge[:, :, :], AX.X, ALU.add),
           reads=(r_g,), writes=(r_g,))
    c.step("vector", lambda e: e.reciprocal(gs[:, :], gs[:, :]), reads=(r_g,), writes=(r_g,))
    c.step("vector", lambda e: e.tensor_tensor(
        ge[:, :, :], ge[:, :, :], gs[:, :].unsqueeze(2).broadcast_to([128, 4, 8]), ALU.mult),
        reads=(r_g,), writes=(r_g,))
    c.step("vector", lambda e: e.tensor_copy(pw[:, :, :], ge[:, :, :]), reads=(r_g,), writes=(r_g,))
    for _ in range(topk - 1):
        c.step("vector", lambda e: e.tensor_reduce(mv[:, :], pw[:, :, :], AX.X, ALU.max),
               reads=(r_g,), writes=(r_g,))
        c.step("vector", lambda e: e.tensor_tensor(
            msk[:, :, :], pw[:, :, :], mv[:, :].unsqueeze(2).broadcast_to([128, 4, 8]),
            ALU.is_ge), reads=(r_g,), writes=(r_g,))
        c.step("vector", lambda e: e.scalar_tensor_tensor(
            pw[:, :, :], msk[:, :, :], -1e30, pw[:, :, :], ALU.mult, ALU.add),
            reads=(r_g,), writes=(r_g,))
    c.step("vector", lambda e: e.tensor_reduce(mv[:, :], pw[:, :, :], AX.X, ALU.max),
           reads=(r_g,), writes=(r_g,))
    c.step("vector", lambda e: e.tensor_tensor(
        msk[:, :, :], ge[:, :, :], mv[:, :].unsqueeze(2).broadcast_to([128, 4, 8]),
        ALU.is_ge), reads=(r_g,), writes=(r_g,))
    c.step("vector", lambda e: e.tensor_tensor(msk[:, :, :], msk[:, :, :], ge[:, :, :], ALU.mult),
           reads=(r_g,), writes=(r_g,))
    c.step("vector", lambda e: e.tensor_reduce(gs[:, :], msk[:, :, :], AX.X, ALU.add),
           reads=(r_g,), writes=(r_g,))
    c.step("vector", lambda e: e.reciprocal(gs[:, :], gs[:, :]), reads=(r_g,), writes=(r_g,))
    c.step("vector", lambda e: e.tensor_tensor(
        msk[:, :, :], msk[:, :, :], gs[:, :].unsqueeze(2).broadcast_to([128, 4, 8]), ALU.mult),
        reads=(r_g,), writes=(r_g,))
    c.rd("tensor", r_g, env["r_consts"])
    c.wr("tensor", bank(1))
    for j in range(4):
        c.emit("tensor", lambda e, j=j: e.transpose(
            ps[0:8, 1, j * 128:(j + 1) * 128], msk[:, j, :],
            cf32[:, cfo["id"]:cfo["id"] + 128]), inc=s["mm"] if j == 3 else None)
    c.did_rd(s["mm"], r_g)
    c.did_wr(s["mm"], bank(1))
    c.step("vector", lambda e, t0=t0: e.tensor_copy(
        AB_CB[0:8, t0:t0 + 512], ps[0:8, 1, 0:512]),
        reads=(bank(1),), writes=(r_cb,))


def emit_l0f(c: Ctx):
    s = c.s
    env = c.env
    ps = env["ps"]
    A32 = env["A32"]
    AB = env["AB"]
    SM = env["SM32"]
    SMB = env["SMB"]
    cbf = env["cbf"]
    co = env["cbf_off"]
    cf32 = env["cf32"]
    cfo = env["cf32_off"]
    R = env["R"]

    def res(name):
        if name not in R:
            R[name] = Res()
        return R[name]

    def bank(i):
        return res(("bank", i))

    x_in = env["l0f_x_in"]
    x_out = env["l0f_x_out"]
    h1_s = env["l0f_h1"]
    wL = env["l0f_wL"]
    wgate = env["l0f_wgate"]
    wexp = env["l0f_wexp"]
    bias = env["l0f_bias"]

    xr = x_in.rearrange("(c p) t -> p c t", p=128)
    xor_ = x_out.rearrange("(c p) t -> p c t", p=128)
    h1r = h1_s.rearrange("(c p) t -> p c t", p=128)

    AB_W = AB[:, 0:24576]
    AB_GW = AB[:, 24576:24640]
    AB_LN = AB[:, 24640:28736]
    AB_H0 = AB[:, 28736:32832]
    AB_SA = AB[:, 32832:36928]
    AB_BB = AB[:, 36928:41024]
    AB_H1 = AB[:, 41024:45120]
    AB_CB = AB[:, 49152:53248]
    AB_HH0 = AB[:, 24576:28672]
    AB_HH1 = AB[:, 28672:32768]
    AB_HE = AB[:, 32768:36864]
    AB_BT = AB[:, 36864:40960]
    AB_CBS = AB[:, 40960:43008]
    A_ACC = A32[:, 0:16384]
    A_X = A32[:, 16384:20480]
    A_T0 = A32[:, 20480:20992]
    A_G = A32[:, 20992:22528]

    r_w = res("AB_W")
    r_gw = res("AB_GW")
    r_bias = res("SM_BIAS")
    r_x = res("A_X")
    r_stats = res("SM_STATS")
    r_ln = res("AB_LN")
    r_h0 = res("AB_H0")
    r_sa = res("AB_SA")
    r_bb = res("AB_BB")
    r_h1 = res("AB_H1")
    r_cb = res("AB_CB")
    r_g = res("A_G")
    r_h1s = res("HBM_l0f_h1")
    r_hh = [res("AB_HH0"), res("AB_HH1")]
    r_he = res("AB_HE")
    r_bt = res("AB_BT")
    r_cbs = res("AB_CBS")
    r_acc = res("A_ACC")
    r_h2b = res("AB_HH0_dup")
    r_xin = res(("hbm", x_in.name))
    r_xout = res(("hbm", x_out.name))

    ln_eps = 1e-5

    c.dma("gpsimd", AB_W, wL[:, 0:24576], reads=(), writes=(r_w, res("AB_W1"), res("AB_W3"), res("AB_W2")))
    c.dma("gpsimd", AB_GW, wgate[:, :], writes=(r_gw,))
    c.dma("sync", SM[:, 0:48], bias[:, :], writes=(r_bias,))

    for n in range(NT):
        t0 = n * TILE
        xt = A_X.rearrange("p (c t) -> p c t", c=8)
        c.dma("sync", xt[:, :, :], xr[:, :, t0:t0 + TILE], reads=(r_xin,), writes=(r_x,))

        xb = AB_H0.rearrange("p (c t) -> p c t", c=8)
        c.step("gpsimd", lambda e: e.tensor_copy(xb[:, :, :], xt[:, :, :]),
               reads=(r_x,), writes=(r_h0,))
        xsq = AB_SA.rearrange("p (c t) -> p c t", c=8)
        c.step("scalar", lambda e: e.activation(out=xsq[:, :, :], in_=xt[:, :, :],
                                                func=AF.Square),
               reads=(r_x,), writes=(r_sa,))

        c.mm_group([(ps[0:1, 0, :], cbf[:, co["cmean"]:co["cmean"] + 1], xb[:, k, :])
                    for k in range(8)], bank(0), reads=(r_h0, env["r_consts"]))
        c.mm_group([(ps[0:1, 1, :], cbf[:, co["cmean"]:co["cmean"] + 1], xsq[:, k, :])
                    for k in range(8)], bank(1), reads=(r_sa, env["r_consts"]))

        m_sb = SM[0:1, 64:576]
        var_sb = SM[0:1, 576:1088]
        rstd_sb = SM[0:1, 1088:1600]
        mb_bf = SMB[0:1, 0:512]
        rstdb_bf = SMB[0:1, 512:1024]
        c.step("vector", lambda e: e.tensor_copy(m_sb, ps[0:1, 0, :]),
               reads=(bank(0),), writes=(r_stats,))
        c.step("vector", lambda e: e.tensor_mul(var_sb, m_sb, m_sb),
               reads=(r_stats,), writes=(r_stats,))
        c.step("vector", lambda e: e.tensor_tensor(var_sb, ps[0:1, 1, :], var_sb, ALU.subtract),
               reads=(bank(1), r_stats), writes=(r_stats,))
        c.step("scalar", lambda e: e.activation(out=rstd_sb, in_=var_sb, func=AF.Sqrt, bias=cf32[0:1, cfo['eps_ln']:cfo['eps_ln'] + 1]),
               reads=(r_stats,), writes=(r_stats,))
        c.step("vector", lambda e: e.reciprocal(rstd_sb, rstd_sb),
               reads=(r_stats,), writes=(r_stats,))
        c.step("vector", lambda e: e.tensor_copy(mb_bf, m_sb), reads=(r_stats,), writes=(r_stats,))
        c.step("vector", lambda e: e.tensor_copy(rstdb_bf, rstd_sb), reads=(r_stats,), writes=(r_stats,))

        c.mm_group([(ps[:, 0, :], cbf[0:1, co["ones_row"]:co["ones_row"] + 128], mb_bf)],
                   bank(0), reads=(r_stats, env["r_consts"]))
        c.mm_group([(ps[:, 1, :], cbf[0:1, co["ones_row"]:co["ones_row"] + 128], rstdb_bf)],
                   bank(1), reads=(r_stats, env["r_consts"]))

        lnb = AB_LN.rearrange("p (c t) -> p c t", c=8)
        for k in range(8):
            c.step("vector", lambda e, k=k: e.tensor_tensor(A_T0, xt[:, k, :], ps[:, 0, :], ALU.subtract),
                   reads=(r_x, bank(0)), writes=(res("A_T0"),))
            c.step("vector", lambda e, k=k: e.tensor_tensor(lnb[:, k, :], A_T0, ps[:, 1, :], ALU.mult),
                   reads=(res("A_T0"), bank(1)), writes=(r_ln,))

        w0 = AB_W.rearrange("p (w k m j) -> p w k m j", w=3, k=8, m=8)
        h0b = AB_H0.rearrange("p (c t) -> p c t", c=8)
        for m in range(8):
            b = 2 + (m % 2)
            c.mm_group([(ps[:, b, :], w0[:, 0, k, m, :], lnb[:, k, :]) for k in range(8)],
                       bank(b), reads=(r_ln, res("AB_W1")))
            c.step("scalar", lambda e, m=m, b=b: e.activation(
                out=h0b[:, m, :], in_=ps[:, b, :], func=AF.Identity, bias=SM[:, m:m + 1]),
                reads=(bank(b), r_bias), writes=(r_h0,))

        sab = AB_SA.rearrange("p (c t) -> p c t", c=8)
        bbb = AB_BB.rearrange("p (c t) -> p c t", c=8)
        h1b = AB_H1.rearrange("p (c t) -> p c t", c=8)
        for m in range(8):
            b = 4 + (m % 2)
            c.mm_group([(ps[:, b, :], w0[:, 1, k, m, :], h0b[:, k, :]) for k in range(8)],
                       bank(b), reads=(r_h0, res("AB_W3")))
            c.step("scalar", lambda e, m=m, b=b: e.activation(
                out=sab[:, m, :], in_=ps[:, b, :], func=AF.Silu, bias=SM[:, 8 + m:9 + m]),
                reads=(bank(b), r_bias), writes=(r_sa,))
            b2 = 6 + (m % 2)
            c.mm_group([(ps[:, b2, :], w0[:, 2, k, m, :], h0b[:, k, :]) for k in range(8)],
                       bank(b2), reads=(r_h0, res("AB_W2")))
            c.step("vector", lambda e, m=m, b2=b2: e.tensor_scalar(
                bbb[:, m, :], ps[:, b2, :], SM[:, 16 + m:17 + m], None, ALU.add),
                reads=(bank(b2), r_bias), writes=(r_bb,))
            c.step("gpsimd", lambda e, m=m: e.tensor_mul(h1b[:, m, :], sab[:, m, :], bbb[:, m, :]),
                   reads=(r_sa, r_bb), writes=(r_h1,))

        c.dma("sync", h1r[:, :, t0:t0 + TILE], h1b[:, :, :], reads=(r_h1,),
              writes=(r_h1s,), sem=s["st"])

        gw = AB_GW.rearrange("p (k m) -> p k m", k=8)
        emit_gating(c, env, t0,
                    [(ps[0:8, 0, :], gw[:, k, :], h1b[:, k, :]) for k in range(8)],
                    (r_h1, r_gw), topk=4)

    W6 = AB[:, 0:24576]
    HHR = AB[:, 24576:40960]
    HE2 = AB[:, 40960:43008]
    BT2 = AB[:, 43008:45056]
    CBS2 = AB[:, 45056:47104]
    r_w6 = [res(("W6", i)) for i in range(2)]
    r_hhr = res("AB_HHR")
    r_he2 = res("AB_HE2")
    r_bt2 = res("AB_BT2")
    r_cbs2 = res("AB_CBS2")
    r_h2b = res("AB_H2B")
    r_h3b = res("AB_H3B")
    hhv = HHR.rearrange("p (n k t) -> p n k t", n=4, k=8)
    first_he2 = [True]
    first_bt2 = [True]
    first_cbs2 = [True]
    for half in range(2):
        ht0 = half * 2048
        out_acc = A_ACC.rearrange("p (m t) -> p m t", m=8)
        for nn in range(4):
            t0 = ht0 + nn * 512
            if half == 0 and nn == 0:
                wr_extra = (r_gw, r_ln, r_h0, r_sa, r_bb)
            elif half == 1 and nn == 0:
                wr_extra = (r_h2b, r_h3b)
            else:
                wr_extra = ()
            c.dma("sync", hhv[:, nn, :, :], h1r[:, :, t0:t0 + 512],
                  reads=(r_h1s,), writes=(r_hhr,) + wr_extra)

        for e_i in range(10):
            scale_c = e_i < 8
            if scale_c:
                sel = cbf[0:8, co["sel8"] + e_i * 128: co["sel8"] + (e_i + 1) * 128]
                for j in range(4):
                    c.mm_group([(ps[:, 4, :], sel,
                                 AB_CB[0:8, ht0 + j * 512:ht0 + (j + 1) * 512])],
                               bank(4), reads=(r_cb, env["r_consts"]))
                    wr_extra = (r_h1,) if first_cbs2[0] else ()
                    first_cbs2[0] = False
                    c.step("vector", lambda e, j=j: e.tensor_copy(
                        CBS2[:, j * 512:(j + 1) * 512], ps[:, 4, :]),
                        reads=(bank(4),), writes=(r_cbs2,) + wr_extra)
            for hhalf in range(2):
                it = e_i * 2 + hhalf
                slot = it % 2
                wb = W6[:, slot * 12288:(slot + 1) * 12288]
                rw = r_w6[slot]
                if half == 0 and it == 0:
                    wr_extra = (res("AB_W1"), res("AB_W3"), r_w)
                elif half == 0 and it == 1:
                    wr_extra = (res("AB_W3"), res("AB_W2"), r_w)
                else:
                    wr_extra = ()
                c.dma("gpsimd", wb[:, 0:4096], wexp[it, 0, :, :], writes=(rw,) + wr_extra)
                c.dma("gpsimd", wb[:, 4096:8192], wexp[it, 1, :, :], writes=(rw,) + wr_extra)
                c.dma("gpsimd", wb[:, 8192:12288], wexp[it, 2, :, :], writes=(rw,) + wr_extra)
                w1v = wb[:, 0:4096].rearrange("p (k m j) -> p k m j", k=8, m=4)
                w3v = wb[:, 4096:8192].rearrange("p (k m j) -> p k m j", k=8, m=4)
                w2v = wb[:, 8192:12288].rearrange("p (k m j) -> p k m j", k=4, m=8)
                heb = HE2.rearrange("p (m t) -> p m t", m=4)
                bbt = BT2.rearrange("p (m t) -> p m t", m=4)
                for nn in range(4):
                    for m in range(4):
                        b = m % 2
                        c.mm_group([(ps[:, b, :], w1v[:, k, m, :], hhv[:, nn, k, :])
                                    for k in range(8)], bank(b), reads=(r_hhr, rw))
                        wr_extra = (r_bb, r_h1) if first_he2[0] else ()
                        first_he2[0] = False
                        c.step("scalar", lambda e, m=m, b=b: e.activation(
                            out=heb[:, m, :], in_=ps[:, b, :], func=AF.Silu),
                            reads=(bank(b),), writes=(r_he2,) + wr_extra)
                        b2 = 2 + (m % 2)
                        c.mm_group([(ps[:, b2, :], w3v[:, k, m, :], hhv[:, nn, k, :])
                                    for k in range(8)], bank(b2), reads=(r_hhr, rw))
                        wr_extra = (r_h1,) if first_bt2[0] else ()
                        first_bt2[0] = False
                        if scale_c:
                            c.step("vector", lambda e, m=m, b2=b2, nn=nn: e.tensor_tensor(
                                bbt[:, m, :], ps[:, b2, :], CBS2[:, nn * 512:(nn + 1) * 512],
                                ALU.mult), reads=(bank(b2), r_cbs2), writes=(r_bt2,) + wr_extra)
                        else:
                            c.step("vector", lambda e, m=m, b2=b2: e.tensor_copy(
                                bbt[:, m, :], ps[:, b2, :]), reads=(bank(b2),),
                                writes=(r_bt2,) + wr_extra)
                        c.step("vector", lambda e, m=m: e.tensor_mul(
                            heb[:, m, :], heb[:, m, :], bbt[:, m, :]),
                            reads=(r_he2, r_bt2), writes=(r_he2,))
                    for mo in range(8):
                        b = 5 + (mo % 3)
                        c.mm_group([(ps[:, b, :], w2v[:, k, mo, :], heb[:, k, :])
                                    for k in range(4)], bank(b), reads=(r_he2, rw))
                        if it == 0:
                            c.step("vector", lambda e, mo=mo, nn=nn, b=b: e.tensor_copy(
                                out_acc[:, mo, nn * 512:(nn + 1) * 512], ps[:, b, :]),
                                reads=(bank(b),), writes=(r_acc,))
                        else:
                            c.step("vector", lambda e, mo=mo, nn=nn, b=b: e.tensor_tensor(
                                out_acc[:, mo, nn * 512:(nn + 1) * 512], ps[:, b, :],
                                out_acc[:, mo, nn * 512:(nn + 1) * 512], ALU.add),
                                reads=(bank(b), r_acc), writes=(r_acc,))

        c.dma("gpsimd", W6, wL[:, 24576:49152], writes=(r_w, r_w6[0], r_w6[1]))
        wp3 = W6.rearrange("p (w k m j) -> p w k m j", w=3, k=8, m=8)
        for nn in range(4):
            t0 = ht0 + nn * 512
            h2b = HHR[:, 0:4096].rearrange("p (k t) -> p k t", k=8)
            c.step("gpsimd", lambda e, nn=nn: e.tensor_copy(
                h2b[:, :, :], out_acc[:, :, nn * 512:(nn + 1) * 512]),
                reads=(r_acc,), writes=(r_h2b,) + ((r_hhr,) if nn == 0 else ()))
            h3b = HHR[:, 4096:8192].rearrange("p (k t) -> p k t", k=8)
            for m in range(8):
                b = m % 2
                c.mm_group([(ps[:, b, :], wp3[:, 0, k, m, :], h2b[:, k, :]) for k in range(8)],
                           bank(b), reads=(r_h2b, r_w, r_w6[0], r_w6[1]))
                c.step("scalar", lambda e, m=m, b=b: e.activation(
                    out=h3b[:, m, :], in_=ps[:, b, :], func=AF.Identity, bias=SM[:, 24 + m:25 + m]),
                    reads=(bank(b), r_bias), writes=(r_h3b,))
            xt = A_X.rearrange("p (c t) -> p c t", c=8)
            c.dma("sync", xt[:, :, :], xr[:, :, t0:t0 + TILE], reads=(r_xin,), writes=(r_x,))
            sa2v = AB[:, 43008:47104].rearrange("p (m t) -> p m t", m=8)
            for m in range(8):
                b = 2 + (m % 2)
                c.mm_group([(ps[:, b, :], wp3[:, 1, k, m, :], h3b[:, k, :]) for k in range(8)],
                           bank(b), reads=(r_h3b, r_w, r_w6[0], r_w6[1]))
                c.step("scalar", lambda e, m=m, b=b: e.activation(
                    out=sa2v[:, m, :], in_=ps[:, b, :], func=AF.Silu, bias=SM[:, 32 + m:33 + m]),
                    reads=(bank(b), r_bias), writes=(r_bt2, r_cbs2))
                b2 = 4 + (m % 2)
                c.mm_group([(ps[:, b2, :], wp3[:, 2, k, m, :], h3b[:, k, :]) for k in range(8)],
                           bank(b2), reads=(r_h3b, r_w, r_w6[0], r_w6[1]))
                c.step("vector", lambda e, m=m, b2=b2: e.tensor_scalar(
                    A_T0, ps[:, b2, :], SM[:, 40 + m:41 + m], None, ALU.add),
                    reads=(bank(b2), r_bias), writes=(res("A_T0"),))
                c.step("vector", lambda e, m=m: e.tensor_tensor(
                    A_T0, A_T0, sa2v[:, m, :], ALU.mult),
                    reads=(res("A_T0"), r_bt2, r_cbs2), writes=(res("A_T0"),))
                c.step("vector", lambda e, m=m: e.tensor_tensor(
                    xt[:, m, :], xt[:, m, :], A_T0, ALU.add),
                    reads=(res("A_T0"), r_x), writes=(r_x,))
            c.dma("sync", xor_[:, :, t0:t0 + TILE], xt[:, :, :], reads=(r_x,),
                  writes=(r_xout,), sem=s["st"])
    if env.get("dev_dump"):
        c.dma("sync", env["l0f_cdump"][:, :], AB_CB[0:8, :], reads=(r_cb,), sem=s["st"])
        c.dma("sync", env["l0f_gdump"][:, :], A_G[:, :], reads=(r_g,), sem=s["st"])


def emit_l1f(c: Ctx):
    s = c.s
    env = c.env
    ps = env["ps"]
    A32 = env["A32"]
    AB = env["AB"]
    SM = env["SM32"]
    SMB = env["SMB"]
    cbf = env["cbf"]
    co = env["cbf_off"]
    cf32 = env["cf32"]
    cfo = env["cf32_off"]
    R = env["R"]

    def res(name):
        if name not in R:
            R[name] = Res()
        return R[name]

    def bank(i):
        return res(("bank", i))

    x_in = env["l1f_x_in"]
    x_out = env["l1f_x_out"]
    ln_s = env["l1f_ln"]
    wgate = env["l1f_wgate"]
    gbias = env["l1f_gbias"]
    wexp = env["l1f_wexp"]

    xr = x_in.rearrange("(c p) t -> p c t", p=128)
    xor_ = x_out.rearrange("(c p) t -> p c t", p=128)
    lnr = ln_s.rearrange("(c p) t -> p c t", p=128)

    AB_W = AB[:, 0:24576]
    AB_GW = AB[:, 24576:24640]
    AB_LN = AB[:, 24640:28736]
    AB_HH0 = AB[:, 24576:28672]
    AB_HH1 = AB[:, 28672:32768]
    AB_HE = AB[:, 32768:36864]
    AB_BT = AB[:, 36864:40960]
    AB_CBS = AB[:, 40960:43008]
    AB_CB = AB[:, 49152:53248]
    A_ACC = A32[:, 0:16384]
    A_X = A32[:, 16384:20480]
    A_T0 = A32[:, 20480:20992]
    A_G = A32[:, 20992:22528]

    r_gw = res("AB_GW")
    r_gb = res("SM_GB")
    r_x = res("A_X")
    r_stats = res("SM_STATS")
    r_ln = res("AB_LN")
    r_lns = res("HBM_l1f_ln")
    r_cb = res("AB_CB")
    r_hh = [res("AB_HH0"), res("AB_HH1")]
    r_he = res("AB_HE")
    r_bt = res("AB_BT")
    r_cbs = res("AB_CBS")
    r_acc = res("A_ACC")
    r_xin = res(("hbm", x_in.name))
    r_xout = res(("hbm", x_out.name))

    c.dma("gpsimd", AB_GW, wgate[:, :], writes=(r_gw, r_hh[0]))
    c.dma("sync", SM[:, 56:57], gbias[:, :], writes=(r_gb,))

    for n in range(NT):
        t0 = n * TILE
        xt = A_X.rearrange("p (c t) -> p c t", c=8)
        c.dma("sync", xt[:, :, :], xr[:, :, t0:t0 + TILE], reads=(r_xin,), writes=(r_x,))

        xb = AB_HE.rearrange("p (c t) -> p c t", c=8)
        c.step("gpsimd", lambda e: e.tensor_copy(xb[:, :, :], xt[:, :, :]),
               reads=(r_x,), writes=(r_he,))
        xsq = AB_BT.rearrange("p (c t) -> p c t", c=8)
        c.step("scalar", lambda e: e.activation(out=xsq[:, :, :], in_=xt[:, :, :],
                                                func=AF.Square),
               reads=(r_x,), writes=(r_bt,))
        c.mm_group([(ps[0:1, 0, :], cbf[:, co["cmean"]:co["cmean"] + 1], xb[:, k, :])
                    for k in range(8)], bank(0), reads=(r_he, env["r_consts"]))
        c.mm_group([(ps[0:1, 1, :], cbf[:, co["cmean"]:co["cmean"] + 1], xsq[:, k, :])
                    for k in range(8)], bank(1), reads=(r_bt, env["r_consts"]))
        m_sb = SM[0:1, 64:576]
        var_sb = SM[0:1, 576:1088]
        rstd_sb = SM[0:1, 1088:1600]
        mb_bf = SMB[0:1, 0:512]
        rstdb_bf = SMB[0:1, 512:1024]
        c.step("vector", lambda e: e.tensor_copy(m_sb, ps[0:1, 0, :]),
               reads=(bank(0),), writes=(r_stats,))
        c.step("vector", lambda e: e.tensor_mul(var_sb, m_sb, m_sb),
               reads=(r_stats,), writes=(r_stats,))
        c.step("vector", lambda e: e.tensor_tensor(var_sb, ps[0:1, 1, :], var_sb, ALU.subtract),
               reads=(bank(1), r_stats), writes=(r_stats,))
        c.step("scalar", lambda e: e.activation(out=rstd_sb, in_=var_sb, func=AF.Sqrt,
                                                bias=cf32[0:1, cfo["eps_ln"]:cfo["eps_ln"] + 1]),
               reads=(r_stats,), writes=(r_stats,))
        c.step("vector", lambda e: e.reciprocal(rstd_sb, rstd_sb),
               reads=(r_stats,), writes=(r_stats,))
        c.step("vector", lambda e: e.tensor_copy(mb_bf, m_sb), reads=(r_stats,), writes=(r_stats,))
        c.step("vector", lambda e: e.tensor_copy(rstdb_bf, rstd_sb), reads=(r_stats,), writes=(r_stats,))
        c.mm_group([(ps[:, 0, :], cbf[0:1, co["ones_row"]:co["ones_row"] + 128], mb_bf)],
                   bank(0), reads=(r_stats, env["r_consts"]))
        c.mm_group([(ps[:, 1, :], cbf[0:1, co["ones_row"]:co["ones_row"] + 128], rstdb_bf)],
                   bank(1), reads=(r_stats, env["r_consts"]))
        lnb = AB_LN.rearrange("p (c t) -> p c t", c=8)
        for k in range(8):
            c.step("vector", lambda e, k=k: e.tensor_tensor(A_T0, xt[:, k, :], ps[:, 0, :], ALU.subtract),
                   reads=(r_x, bank(0)), writes=(res("A_T0"),))
            c.step("vector", lambda e, k=k: e.tensor_tensor(lnb[:, k, :], A_T0, ps[:, 1, :], ALU.mult),
                   reads=(res("A_T0"), bank(1)), writes=(r_ln,))
        c.dma("sync", lnr[:, :, t0:t0 + TILE], lnb[:, :, :], reads=(r_ln,),
              writes=(r_lns,), sem=s["st"])
        gw = AB_GW.rearrange("p (k m) -> p k m", k=8)
        emit_gating(c, env, t0,
                    [(ps[0:8, 0, :], gw[:, k, :], lnb[:, k, :]) for k in range(8)],
                    (r_ln, r_gw), topk=2, bias_ap=SM[0:8, 56:57], bias_res=r_gb)

    lnr2 = lnr
    W6 = AB[:, 0:24576]
    HHR = AB[:, 24576:40960]
    HE2 = AB[:, 40960:43008]
    BT2 = AB[:, 43008:45056]
    CBS2 = AB[:, 45056:47104]
    r_w6 = [res(("W6", i)) for i in range(2)]
    r_hhr = res("AB_HHR")
    r_he2 = res("AB_HE2")
    r_bt2 = res("AB_BT2")
    r_cbs2 = res("AB_CBS2")
    hhv = HHR.rearrange("p (n k t) -> p n k t", n=4, k=8)
    for half in range(2):
        ht0 = half * 2048
        out_acc = A_ACC.rearrange("p (m t) -> p m t", m=8)
        for nn in range(4):
            t0 = ht0 + nn * 512
            wr_extra = (r_gw, r_ln, r_he, r_bt) if (half == 0 and nn == 0) else ()
            c.dma("sync", hhv[:, nn, :, :], lnr2[:, :, t0:t0 + 512],
                  reads=(r_lns,), writes=(r_hhr,) + wr_extra)

        for pe in range(32):
            e_i = pe // 4
            if pe % 4 == 0:
                sel = cbf[0:8, co["sel8"] + e_i * 128: co["sel8"] + (e_i + 1) * 128]
                for j in range(4):
                    c.mm_group([(ps[:, 4, :], sel,
                                 AB_CB[0:8, ht0 + j * 512:ht0 + (j + 1) * 512])],
                               bank(4), reads=(r_cb, env["r_consts"]))
                    c.step("vector", lambda e, j=j: e.tensor_copy(
                        CBS2[:, j * 512:(j + 1) * 512], ps[:, 4, :]),
                        reads=(bank(4),), writes=(r_cbs2,))
            for hhalf in range(2):
                it = pe * 2 + hhalf
                slot = it % 2
                wb = W6[:, slot * 12288:(slot + 1) * 12288]
                rw = r_w6[slot]
                c.dma("gpsimd", wb[:, 0:4096], wexp[it, 0, :, :], writes=(rw,))
                c.dma("gpsimd", wb[:, 4096:8192], wexp[it, 1, :, :], writes=(rw,))
                c.dma("gpsimd", wb[:, 8192:12288], wexp[it, 2, :, :], writes=(rw,))
                w1v = wb[:, 0:4096].rearrange("p (k m j) -> p k m j", k=8, m=4)
                w3v = wb[:, 4096:8192].rearrange("p (k m j) -> p k m j", k=8, m=4)
                w2v = wb[:, 8192:12288].rearrange("p (k m j) -> p k m j", k=4, m=8)
                heb = HE2.rearrange("p (m t) -> p m t", m=4)
                bbt = BT2.rearrange("p (m t) -> p m t", m=4)
                for nn in range(4):
                    for m in range(4):
                        b = m % 2
                        c.mm_group([(ps[:, b, :], w1v[:, k, m, :], hhv[:, nn, k, :])
                                    for k in range(8)], bank(b), reads=(r_hhr, rw))
                        c.step("scalar", lambda e, m=m, b=b: e.activation(
                            out=heb[:, m, :], in_=ps[:, b, :], func=AF.Silu),
                            reads=(bank(b),), writes=(r_he2,))
                        b2 = 2 + (m % 2)
                        c.mm_group([(ps[:, b2, :], w3v[:, k, m, :], hhv[:, nn, k, :])
                                    for k in range(8)], bank(b2), reads=(r_hhr, rw))
                        c.step("vector", lambda e, m=m, b2=b2, nn=nn: e.tensor_tensor(
                            bbt[:, m, :], ps[:, b2, :], CBS2[:, nn * 512:(nn + 1) * 512],
                            ALU.mult), reads=(bank(b2), r_cbs2), writes=(r_bt2,))
                        c.step("vector", lambda e, m=m: e.tensor_mul(
                            heb[:, m, :], heb[:, m, :], bbt[:, m, :]),
                            reads=(r_he2, r_bt2), writes=(r_he2,))
                    for mo in range(8):
                        b = 5 + (mo % 3)
                        c.mm_group([(ps[:, b, :], w2v[:, k, mo, :], heb[:, k, :])
                                    for k in range(4)], bank(b), reads=(r_he2, rw))
                        if it == 0:
                            c.step("vector", lambda e, mo=mo, nn=nn, b=b: e.tensor_copy(
                                out_acc[:, mo, nn * 512:(nn + 1) * 512], ps[:, b, :]),
                                reads=(bank(b),), writes=(r_acc,))
                        else:
                            c.step("vector", lambda e, mo=mo, nn=nn, b=b: e.tensor_tensor(
                                out_acc[:, mo, nn * 512:(nn + 1) * 512], ps[:, b, :],
                                out_acc[:, mo, nn * 512:(nn + 1) * 512], ALU.add),
                                reads=(bank(b), r_acc), writes=(r_acc,))

        for nn in range(4):
            t0 = ht0 + nn * 512
            xt = A_X.rearrange("p (c t) -> p c t", c=8)
            c.dma("sync", xt[:, :, :], xr[:, :, t0:t0 + TILE], reads=(r_xin,), writes=(r_x,))
            for m in range(8):
                c.step("vector", lambda e, m=m, nn=nn: e.tensor_tensor(
                    xt[:, m, :], xt[:, m, :], out_acc[:, m, nn * 512:(nn + 1) * 512],
                    ALU.add), reads=(r_acc, r_x), writes=(r_x,))
            c.dma("sync", xor_[:, :, t0:t0 + TILE], xt[:, :, :], reads=(r_x,),
                  writes=(r_xout,), sem=s["st"])


def emit_l0a(c: Ctx):
    s = c.s
    env = c.env
    ps = env["ps"]
    A32 = env["A32"]
    AB = env["AB"]
    SM = env["SM32"]
    SMB = env["SMB"]
    cbf = env["cbf"]
    co = env["cbf_off"]
    cf32 = env["cf32"]
    cfo = env["cf32_off"]
    R = env["R"]
    RS = float(1.0 / np.sqrt(2.0))

    def res(name):
        if name not in R:
            R[name] = Res()
        return R[name]

    def bank(i):
        return res(("bank", i))

    def aregs(start, width):
        return tuple(res(("A32blk", b)) for b in range(start // 4096, (start + width - 1) // 4096 + 1))

    def bregs(start, width):
        return tuple(res(("ABblk", b)) for b in range(start // 4096, (start + width - 1) // 4096 + 1))

    x_in = env["l0a_x_in"]
    x_out = env["l0a_x_out"]
    wip = env["l0a_wip"]
    wop = env["l0a_wop"]
    wfc = env["l0a_wfc"]
    wpj = env["l0a_wpj"]
    n1 = env["l0a_n1"]
    n2 = env["l0a_n2"]
    smask = env["smask"]
    prevr = env["prev"]
    h1s = env["l0a_h1"]
    qns = env["l0a_qn"]
    us = env["l0a_u"]
    gs_ = env["l0a_gate"]
    vs = env["l0a_v"]
    as_ = env["l0a_a"]
    Ys = env["l0a_Y"]
    Cs = env["l0a_C"]
    t1s = env["l0a_t1"]
    sfcs = env["l0a_sfc"]
    scon = env["l0a_scontrib"]
    sgat = env["l0a_sgather"]

    def fm(t):
        return t.rearrange("(c p) t -> p c t", p=128)

    xr, h1r, qnr, ur, gr, vr, ar, Yr, Cr, t1r = (
        fm(x_in), fm(h1s), fm(qns), fm(us), fm(gs_), fm(vs), fm(as_), fm(Ys), fm(Cs), fm(t1s))
    xor_ = fm(x_out)
    sfr = sfcs.rearrange("(c p) t -> p c t", p=128)

    r_xin = res(("hbm", x_in.name))
    r_xout = res(("hbm", x_out.name))
    r_h1s = res("HBM_l0a_h1")
    r_qkvag = res("HBM_l0a_qkvag")
    r_Y = res("HBM_l0a_Y")
    r_C = res("HBM_l0a_C")
    r_t1 = res("HBM_l0a_t1")
    r_sfc = res("HBM_l0a_sfc")
    r_coll = res("HBM_l0a_coll")
    r_stats = res("SM_STATS")
    r_small = res("SM_L0A")

    c.dma("sync", SM[:, 16:24], n1[:, :], writes=(r_small,))
    c.dma("sync", SM[:, 24:32], n2[:, :], writes=(r_small,))
    c.dma("sync", SM[:, 32:33], smask[:, :], writes=(r_small,))

    phases = env.get("l0a_phases", ("p0", "p1q", "p1kv", "p1ag", "p2", "p3", "p4"))
    c.dma("gpsimd", env["pt_i32"], prevr[:, :], writes=(res("PT"),))
    for n in range(NT if "p0" in phases else 0):
        t0 = n * TILE
        xt = A32[:, 0:4096].rearrange("p (c t) -> p c t", c=8)
        c.dma("sync", xt[:, :, :], xr[:, :, t0:t0 + TILE], reads=(r_xin,), writes=aregs(0, 4096))
        xb = AB[:, 0:4096].rearrange("p (c t) -> p c t", c=8)
        c.step("gpsimd", lambda e: e.tensor_copy(xb[:, :, :], xt[:, :, :]),
               reads=aregs(0, 4096), writes=bregs(0, 4096))
        xsq = AB[:, 4096:8192].rearrange("p (c t) -> p c t", c=8)
        c.step("scalar", lambda e: e.activation(out=xsq[:, :, :], in_=xt[:, :, :], func=AF.Square),
               reads=aregs(0, 4096), writes=bregs(4096, 4096))
        c.mm_group([(ps[0:1, 0, :], cbf[:, co["cmean"]:co["cmean"] + 1], xb[:, k, :])
                    for k in range(8)], bank(0), reads=bregs(0, 4096) + (env["r_consts"],))
        c.mm_group([(ps[0:1, 1, :], cbf[:, co["cmean"]:co["cmean"] + 1], xsq[:, k, :])
                    for k in range(8)], bank(1), reads=bregs(4096, 4096) + (env["r_consts"],))
        m_sb = SM[0:1, 64:576]
        var_sb = SM[0:1, 576:1088]
        s_sb = SM[0:1, 1088:1600]
        mb_bf = SMB[0:1, 0:512]
        sb_bf = SMB[0:1, 512:1024]
        c.step("vector", lambda e: e.tensor_copy(m_sb, ps[0:1, 0, :]),
               reads=(bank(0),), writes=(r_stats,))
        c.step("vector", lambda e: e.tensor_mul(var_sb, m_sb, m_sb),
               reads=(r_stats,), writes=(r_stats,))
        c.step("vector", lambda e: e.tensor_tensor(var_sb, ps[0:1, 1, :], var_sb, ALU.subtract),
               reads=(bank(1), r_stats), writes=(r_stats,))
        ve_sb = SM[0:1, 40:44] if False else None
        c.step("vector", lambda e: e.tensor_scalar(s_sb, var_sb, 1e-5, None, ALU.add),
               reads=(r_stats,), writes=(r_stats,))
        c.step("vector", lambda e: e.reciprocal(s_sb, s_sb),
               reads=(r_stats,), writes=(r_stats,))
        c.step("vector", lambda e: e.tensor_tensor(s_sb, var_sb, s_sb, ALU.mult),
               reads=(r_stats,), writes=(r_stats,))
        c.step("vector", lambda e: e.tensor_scalar(s_sb, s_sb, 1e-6, None, ALU.add),
               reads=(r_stats,), writes=(r_stats,))
        c.step("vector", lambda e: e.tensor_scalar(var_sb, var_sb, 1e-5, None, ALU.add),
               reads=(r_stats,), writes=(r_stats,))
        c.step("vector", lambda e: e.tensor_tensor(s_sb, s_sb, var_sb, ALU.mult),
               reads=(r_stats,), writes=(r_stats,))
        c.step("scalar", lambda e: e.activation(out=s_sb, in_=s_sb, func=AF.Sqrt),
               reads=(r_stats,), writes=(r_stats,))
        c.step("vector", lambda e: e.reciprocal(s_sb, s_sb), reads=(r_stats,), writes=(r_stats,))
        c.step("vector", lambda e: e.tensor_copy(mb_bf, m_sb), reads=(r_stats,), writes=(r_stats,))
        c.step("vector", lambda e: e.tensor_copy(sb_bf, s_sb), reads=(r_stats,), writes=(r_stats,))
        c.mm_group([(ps[:, 0, :], cbf[0:1, co["ones_row"]:co["ones_row"] + 128], mb_bf)],
                   bank(0), reads=(r_stats, env["r_consts"]))
        c.mm_group([(ps[:, 1, :], cbf[0:1, co["ones_row"]:co["ones_row"] + 128], sb_bf)],
                   bank(1), reads=(r_stats, env["r_consts"]))
        h1b = AB[:, 8192:12288].rearrange("p (c t) -> p c t", c=8)
        T0 = A32[:, 20480:20992]
        for k in range(8):
            c.step("vector", lambda e, k=k: e.tensor_tensor(T0, xt[:, k, :], ps[:, 0, :], ALU.subtract),
                   reads=aregs(0, 4096) + (bank(0),), writes=aregs(20480, 512))
            c.step("vector", lambda e, k=k: e.tensor_tensor(T0, T0, ps[:, 1, :], ALU.mult),
                   reads=aregs(20480, 512) + (bank(1),), writes=aregs(20480, 512))
            c.step("vector", lambda e, k=k: e.tensor_scalar(h1b[:, k, :], T0, SM[:, 16 + k:17 + k],
                                                            None, ALU.mult),
                   reads=aregs(20480, 512) + (r_small,), writes=bregs(8192, 4096))
        c.dma("sync", h1r[:, :, t0:t0 + TILE], h1b[:, :, :], reads=bregs(8192, 4096),
              writes=(r_h1s,), sem=s["st"])

    hhR = [bregs(16384, 4096), bregs(20480, 4096)]

    def load_h1_tile(nn, t0):
        hb = nn % 2
        hh = AB[:, 16384 + hb * 4096: 20480 + hb * 4096].rearrange("p (k t) -> p k t", k=8)
        c.dma("gpsimd", hh[:, :, :], h1r[:, :, t0:t0 + 512], reads=(r_h1s,), writes=hhR[hb])
        return hh, hhR[hb]

    c.dma("gpsimd", AB[:, 0:8192], wip[:, 0:8192], writes=bregs(0, 8192))
    wq = AB[:, 0:8192].rearrange("p (k m j) -> p k m j", k=8, m=8)
    for n in range(NT if "p1q" in phases else 0):
        t0 = n * TILE
        hh, hr = load_h1_tile(n, t0)
        qb = AB[:, 8192:12288].rearrange("p (c t) -> p c t", c=8)
        qq = AB[:, 12288:16384].rearrange("p (c t) -> p c t", c=8)
        for m in range(8):
            b = m % 2
            c.mm_group([(ps[:, b, :], wq[:, k, m, :], hh[:, k, :]) for k in range(8)],
                       bank(b), reads=hr + bregs(0, 8192))
            c.step("vector", lambda e, m=m, b=b: e.tensor_copy(qb[:, m, :], ps[:, b, :]),
                   reads=(bank(b),), writes=bregs(8192, 4096))
            c.step("scalar", lambda e, m=m, b=b: e.activation(
                out=qq[:, m, :], in_=qb[:, m, :], func=AF.Square),
                reads=bregs(8192, 4096), writes=bregs(12288, 4096))
        rq = A32[:, 0:512]
        for m in range(8):
            b2 = 2 + (m % 2)
            c.mm_group([(ps[:, b2, :], cbf[:, co["b64"]:co["b64"] + 128], qq[:, m, :])],
                       bank(b2), reads=bregs(12288, 4096) + (env["r_consts"],))
            c.step("scalar", lambda e, b2=b2: e.activation(
                out=rq, in_=ps[:, b2, :], func=AF.Sqrt,
                bias=cf32[:, cfo["eps_fp32"]:cfo["eps_fp32"] + 1]),
                reads=(bank(b2),), writes=aregs(0, 512))
            c.step("vector", lambda e: e.reciprocal(rq, rq),
                   reads=aregs(0, 512), writes=aregs(0, 512))
            c.step("vector", lambda e, m=m: e.tensor_tensor(qb[:, m, :], qb[:, m, :], rq, ALU.mult),
                   reads=aregs(0, 512) + bregs(8192, 4096), writes=bregs(8192, 4096))
        c.dma("sync", qnr[:, :, t0:t0 + TILE], qb[:, :, :], reads=bregs(8192, 4096),
              writes=(r_qkvag,), sem=s["st"])

    c.dma("gpsimd", AB[:, 0:16384], wip[:, 8192:24576], writes=bregs(0, 16384))
    wkv = AB[:, 0:16384].rearrange("p (k m j) -> p k m j", k=8, m=16)
    for n in range(NT if "p1kv" in phases else 0):
        t0 = n * TILE
        hh, hr = load_h1_tile(n, t0)
        kt = AB[:, 24576:28672].rearrange("p (c t) -> p c t", c=8)
        vb = AB[:, 28672:32768].rearrange("p (c t) -> p c t", c=8)
        for m in range(8):
            b = m % 2
            c.mm_group([(ps[:, b, :], wkv[:, k, m, :], hh[:, k, :]) for k in range(8)],
                       bank(b), reads=hr + bregs(0, 16384))
            c.step("scalar", lambda e, m=m, b=b: e.activation(
                out=kt[:, m, :], in_=ps[:, b, :], func=AF.Tanh),
                reads=(bank(b),), writes=bregs(24576, 4096))
            b2 = 2 + (m % 2)
            c.mm_group([(ps[:, b2, :], wkv[:, k, 8 + m, :], hh[:, k, :]) for k in range(8)],
                       bank(b2), reads=hr + bregs(0, 16384))
            c.step("vector", lambda e, m=m, b2=b2: e.tensor_copy(vb[:, m, :], ps[:, b2, :]),
                   reads=(bank(b2),), writes=bregs(28672, 4096))
            c.step("gpsimd", lambda e, m=m: e.tensor_mul(kt[:, m, :], kt[:, m, :], vb[:, m, :]),
                   reads=bregs(24576, 4096) + bregs(28672, 4096), writes=bregs(24576, 4096))
        c.dma("sync", ur[:, :, t0:t0 + TILE], kt[:, :, :], reads=bregs(24576, 4096),
              writes=(r_qkvag,), sem=s["st"])
        c.dma("scalar", vr[:, :, t0:t0 + TILE], vb[:, :, :], reads=bregs(28672, 4096),
              writes=(r_qkvag,), sem=s["st"])

    c.dma("gpsimd", AB[:, 0:16384], wip[:, 24576:40960], writes=bregs(0, 16384))
    wag = AB[:, 0:16384].rearrange("p (k m j) -> p k m j", k=8, m=16)
    for n in range(NT if "p1ag" in phases else 0):
        t0 = n * TILE
        hh, hr = load_h1_tile(n, t0)
        af = A32[:, 0:4096].rearrange("p (c t) -> p c t", c=8)
        gb = AB[:, 24576:28672].rearrange("p (c t) -> p c t", c=8)
        for m in range(8):
            b = m % 2
            c.mm_group([(ps[:, b, :], wag[:, k, m, :], hh[:, k, :]) for k in range(8)],
                       bank(b), reads=hr + bregs(0, 16384))
            c.step("scalar", lambda e, m=m, b=b: e.activation(
                out=af[:, m, :], in_=ps[:, b, :], func=AF.Sigmoid,
                bias=cf32[:, cfo["bias2"]:cfo["bias2"] + 1]),
                reads=(bank(b),), writes=aregs(0, 4096))
            c.step("vector", lambda e, m=m: e.tensor_scalar(
                af[:, m, :], af[:, m, :], 0.6, 0.9995, ALU.max, ALU.min),
                reads=aregs(0, 4096), writes=aregs(0, 4096))
            b2 = 2 + (m % 2)
            c.mm_group([(ps[:, b2, :], wag[:, k, 8 + m, :], hh[:, k, :]) for k in range(8)],
                       bank(b2), reads=hr + bregs(0, 16384))
            c.step("scalar", lambda e, m=m, b2=b2: e.activation(
                out=gb[:, m, :], in_=ps[:, b2, :], func=AF.Sigmoid),
                reads=(bank(b2),), writes=bregs(24576, 4096))
        c.dma("sync", ar[:, :, t0:t0 + TILE], af[:, :, :], reads=aregs(0, 4096),
              writes=(r_qkvag,), sem=s["st"])
        c.dma("scalar", gr[:, :, t0:t0 + TILE], gb[:, :, :], reads=bregs(24576, 4096),
              writes=(r_qkvag,), sem=s["st"])

    c.dma("gpsimd", AB[:, 0:8192], wop[:, :], writes=bregs(0, 8192))
    wopv = AB[:, 0:8192].rearrange("p (k m j) -> p k m j", k=8, m=8)
    ZER = SM[:, 405:469]
    c.step("gpsimd", lambda e: e.memset(ZER, 0.0), writes=(res("ZER"),))
    R.setdefault("ZER", Res())
    carry_s = A32[:, 16384:16392]
    carry_Q = A32[:, 16392:16400]
    sin9 = A32[:, 16400:16472].rearrange("p (d n) -> p d n", d=8)
    Qin9 = A32[:, 16472:16544].rearrange("p (d n) -> p d n", d=8)
    rcar = aregs(16384, 4096)
    c.step("gpsimd", lambda e: e.memset(carry_s[:, :], 0.0), writes=rcar)
    c.step("gpsimd", lambda e: e.memset(carry_Q[:, :], 1.0), writes=rcar)
    for n in range(NT if "p2" in phases else 0):
        t0 = n * TILE
        af = A32[:, 0:4096].rearrange("p (c t) -> p c t", c=8)
        c.dma("sync", af[:, :, :], ar[:, :, t0:t0 + TILE], reads=(r_qkvag,), writes=aregs(0, 4096))
        ub = AB[:, 8192:12288].rearrange("p (c t) -> p c t", c=8)
        c.dma("sync", ub[:, :, :], ur[:, :, t0:t0 + TILE], reads=(r_qkvag,), writes=bregs(8192, 4096))
        qnb = AB[:, 12288:16384].rearrange("p (c t) -> p c t", c=8)
        c.dma("sync", qnb[:, :, :], qnr[:, :, t0:t0 + TILE], reads=(r_qkvag,), writes=bregs(12288, 4096))
        gb = AB[:, 16384:20480].rearrange("p (c t) -> p c t", c=8)
        c.dma("gpsimd", gb[:, :, :], gr[:, :, t0:t0 + TILE], reads=(r_qkvag,), writes=bregs(16384, 4096))
        vb = AB[:, 20480:24576].rearrange("p (c t) -> p c t", c=8)
        c.dma("gpsimd", vb[:, :, :], vr[:, :, t0:t0 + TILE], reads=(r_qkvag,), writes=bregs(20480, 4096))

        pf = A32[:, 4096:8192].rearrange("p (c t) -> p c t", c=8)
        ra = A32[:, 8192:12288].rearrange("p (c t) -> p c t", c=8)
        mem = A32[:, 12288:16384].rearrange("p (c t) -> p c t", c=8)
        for dc in range(8):
            eng = "vector"
            for ch in range(8):
                c.step(eng, lambda e, dc=dc, ch=ch: e.tensor_tensor_scan(
                    pf[:, dc, ch * 64:(ch + 1) * 64], af[:, dc, ch * 64:(ch + 1) * 64],
                    ZER, 1.0, ALU.mult, ALU.add),
                    reads=aregs(0, 4096) + (res("ZER"),), writes=aregs(4096, 4096))
        c.step("vector", lambda e: e.tensor_scalar(ra[:, :, :], pf[:, :, :], 1e-6, None, ALU.max),
               reads=aregs(4096, 4096), writes=aregs(8192, 4096))
        c.step("scalar", lambda e: e.activation(out=ra[:, :, :], in_=ra[:, :, :], func=AF.Ln),
               reads=aregs(8192, 4096), writes=aregs(8192, 4096))
        c.step("scalar", lambda e: e.activation(out=ra[:, :, :], in_=ra[:, :, :], func=AF.Exp,
                                                scale=-1.0),
               reads=aregs(8192, 4096), writes=aregs(8192, 4096))
        c.step("vector", lambda e: e.tensor_tensor(ra[:, :, :], ub[:, :, :], ra[:, :, :], ALU.mult),
               reads=bregs(8192, 4096) + aregs(8192, 4096), writes=aregs(8192, 4096))
        for dc in range(8):
            eng = "vector"
            for ch in range(8):
                c.step(eng, lambda e, dc=dc, ch=ch: e.tensor_tensor_scan(
                    ra[:, dc, ch * 64:(ch + 1) * 64], ra[:, dc, ch * 64:(ch + 1) * 64],
                    ZER, 0.0, ALU.add, ALU.add),
                    reads=aregs(8192, 4096) + (res("ZER"),), writes=aregs(8192, 4096))
        for dc in range(8):
            Pc = A32[:, 16544 + dc * 32: 16552 + dc * 32]
            A63 = A32[:, 16552 + dc * 32: 16560 + dc * 32]
            PB = A32[:, 16560 + dc * 32: 16568 + dc * 32]
            c.step("vector", lambda e, dc=dc, Pc=Pc: e.tensor_copy(
                Pc, pf[:, dc, 63::64]), reads=aregs(4096, 4096), writes=rcar)
            c.step("vector", lambda e, dc=dc, A63=A63: e.tensor_copy(
                A63, ra[:, dc, 63::64]), reads=aregs(8192, 4096), writes=rcar)
            c.step("vector", lambda e, Pc=Pc, A63=A63, PB=PB: e.tensor_tensor(
                PB, Pc, A63, ALU.mult), reads=rcar, writes=rcar)
            c.step("vector", lambda e, dc=dc: e.tensor_copy(sin9[:, dc, 0:1], carry_s[:, dc:dc + 1]),
                   reads=rcar, writes=rcar)
            c.step("vector", lambda e, dc=dc, Pc=Pc, PB=PB: e.tensor_tensor_scan(
                sin9[:, dc, 1:9], Pc, PB, carry_s[:, dc:dc + 1], ALU.mult, ALU.add),
                reads=rcar, writes=rcar)
            c.step("vector", lambda e, dc=dc: e.tensor_copy(carry_s[:, dc:dc + 1], sin9[:, dc, 8:9]),
                   reads=rcar, writes=rcar)
            c.step("vector", lambda e, dc=dc: e.tensor_copy(Qin9[:, dc, 0:1], carry_Q[:, dc:dc + 1]),
                   reads=rcar, writes=rcar)
            c.step("vector", lambda e, dc=dc, Pc=Pc: e.tensor_tensor_scan(
                Qin9[:, dc, 1:9], Pc, ZER[:, 0:8],
                carry_Q[:, dc:dc + 1], ALU.mult, ALU.add),
                reads=rcar + (res("ZER"),), writes=rcar)
            c.step("vector", lambda e, dc=dc: e.tensor_copy(carry_Q[:, dc:dc + 1], Qin9[:, dc, 8:9]),
                   reads=rcar, writes=rcar)
        for dc in range(8):
            mv = mem[:, dc, :].rearrange("p (ch t) -> p ch t", ch=8)
            rv = ra[:, dc, :].rearrange("p (ch t) -> p ch t", ch=8)
            pv = pf[:, dc, :].rearrange("p (ch t) -> p ch t", ch=8)
            c.step("vector", lambda e, dc=dc, mv=mv, rv=rv: e.tensor_tensor(
                mv[:, :, :], rv[:, :, :],
                sin9[:, dc, 0:8].unsqueeze(2).broadcast_to([128, 8, 64]), ALU.add),
                reads=aregs(8192, 4096) + rcar, writes=aregs(12288, 4096))
            c.step("vector", lambda e, mv=mv, pv=pv: e.tensor_tensor(
                mv[:, :, :], mv[:, :, :], pv[:, :, :], ALU.mult),
                reads=aregs(12288, 4096) + aregs(4096, 4096), writes=aregs(12288, 4096))
            c.step("gpsimd", lambda e, dc=dc, pv=pv: e.tensor_tensor(
                pv[:, :, :], pv[:, :, :],
                Qin9[:, dc, 0:8].unsqueeze(2).broadcast_to([128, 8, 64]), ALU.mult),
                reads=aregs(4096, 4096) + rcar, writes=aregs(4096, 4096))
        yb = AB[:, 24576:28672].rearrange("p (c t) -> p c t", c=8)
        Cb = AB[:, 28672:32768].rearrange("p (c t) -> p c t", c=8)
        T0 = A32[:, 20480:20992]
        T1 = A32[:, 20992:21504]
        for dc in range(8):
            c.step("vector", lambda e, dc=dc: e.tensor_tensor(
                T0, gb[:, dc, :], qnb[:, dc, :], ALU.mult),
                reads=bregs(16384, 4096) + bregs(12288, 4096), writes=aregs(20480, 512))
            c.step("vector", lambda e, dc=dc: e.tensor_tensor(
                Cb[:, dc, :], T0, pf[:, dc, :], ALU.mult),
                reads=aregs(20480, 512) + aregs(4096, 4096), writes=bregs(28672, 4096))
            c.step("vector", lambda e, dc=dc: e.tensor_tensor(
                T0, T0, mem[:, dc, :], ALU.mult),
                reads=aregs(20480, 512) + aregs(12288, 4096), writes=aregs(20480, 512))
            c.step("gpsimd", lambda e, dc=dc: e.tensor_tensor(
                T1, gb[:, dc, :], vb[:, dc, :], ALU.mult),
                reads=bregs(16384, 4096) + bregs(20480, 4096), writes=aregs(20992, 512))
            c.step("vector", lambda e: e.tensor_tensor(T0, T0, T1, ALU.subtract),
                   reads=aregs(20480, 512) + aregs(20992, 512), writes=aregs(20480, 512))
            c.step("vector", lambda e, dc=dc: e.tensor_tensor(
                yb[:, dc, :], T0, vb[:, dc, :], ALU.add),
                reads=aregs(20480, 512) + bregs(20480, 4096), writes=bregs(24576, 4096))
        c.dma("scalar", Cr[:, :, t0:t0 + TILE], Cb[:, :, :], reads=bregs(28672, 4096),
              writes=(r_C,), sem=s["st"])
        for m in range(8):
            b = 4 + (m % 2)
            c.mm_group([(ps[:, b, :], wopv[:, k, m, :], yb[:, k, :]) for k in range(8)],
                       bank(b), reads=bregs(24576, 4096) + bregs(0, 8192))
            c.step("vector", lambda e, m=m, b=b: e.tensor_scalar(
                A32[:, 21504 + 0:22016], ps[:, b, :], RS, None, ALU.mult),
                reads=(bank(b),), writes=aregs(21504, 512))
            c.dma("sync", Yr[:, m, t0:t0 + TILE], A32[:, 21504:22016],
                  reads=aregs(21504, 512), writes=(r_Y,), sem=s["st"])

    c.dma("gpsimd", scon.rearrange("r (p d) -> p (r d)", p=128), carry_s[:, :],
          reads=rcar, writes=(r_coll,), sem=s["w"])
    c.rd("gpsimd", r_coll)
    c.emit("gpsimd", lambda e: e.collective_compute(
        "AllGather", ALU.bypass, replica_groups=[list(range(8))],
        ins=[scon.opt()], outs=[sgat.opt()]), inc=s["coll"], inc_n=1)
    c.did_wr(s["coll"], res("HBM_l0a_gat"))
    sin_sb = A32[:, 16384:16392]
    c.rd("scalar", res("HBM_l0a_gat"))
    c.rd("scalar", res("PT"))
    c.wr("scalar", *rcar)
    def _sin_load(e):
        r = e.value_load(env["pt_i32"][0:1, 0:1])
        return e.dma_start(out=sin_sb[:, :],
                           in_=sgat[bass.ds(r, 1), :].rearrange("r (p d) -> p (r d)", p=128))
    c.emit("scalar", _sin_load, inc=s["ld"], inc_n=16)
    c.did_wr(s["ld"], *rcar)
    c.step("vector", lambda e: e.tensor_scalar(sin_sb[:, :], sin_sb[:, :], SM[:, 32:33],
                                               None, ALU.mult),
           reads=rcar + (r_small,), writes=rcar)
    for k in range(8):
        c.step("vector", lambda e, k=k: e.tensor_scalar(
            AB[:, k * 1024:(k + 1) * 1024], AB[:, k * 1024:(k + 1) * 1024],
            sin_sb[:, k:k + 1], None, ALU.mult),
            reads=rcar + bregs(0, 8192), writes=bregs(0, 8192))

    c.dma("gpsimd", AB[:, 16384:49152], wfc[:, :], writes=bregs(16384, 32768))
    wfcv = AB[:, 16384:49152].rearrange("p (k m j) -> p k m j", k=8, m=32)
    for n in range(NT if "p3" in phases else 0):
        t0 = n * TILE
        xt = A32[:, 0:4096].rearrange("p (c t) -> p c t", c=8)
        c.dma("sync", xt[:, :, :], xr[:, :, t0:t0 + TILE], reads=(r_xin,), writes=aregs(0, 4096))
        lnf = A32[:, 4096:8192].rearrange("p (c t) -> p c t", c=8)
        Yt = A32[:, 8192:12288].rearrange("p (c t) -> p c t", c=8)
        c.dma("sync", Yt[:, :, :], Yr[:, :, t0:t0 + TILE], reads=(r_Y,), writes=aregs(8192, 4096))
        t1f = A32[:, 12288:16384].rearrange("p (c t) -> p c t", c=8)
        Cbt = AB[:, 8192:12288].rearrange("p (c t) -> p c t", c=8)
        c.dma("gpsimd", Cbt[:, :, :], Cr[:, :, t0:t0 + TILE], reads=(r_C,), writes=bregs(8192, 4096))
        xb = AB[:, 12288:16384].rearrange("p (c t) -> p c t", c=8)
        c.step("gpsimd", lambda e: e.tensor_copy(xb[:, :, :], xt[:, :, :]),
               reads=aregs(0, 4096), writes=bregs(12288, 4096))
        xsq = AB[:, 49152:53248].rearrange("p (c t) -> p c t", c=8)
        c.step("scalar", lambda e: e.activation(out=xsq[:, :, :], in_=xt[:, :, :], func=AF.Square),
               reads=aregs(0, 4096), writes=bregs(49152, 4096))
        c.mm_group([(ps[0:1, 0, :], cbf[:, co["cmean"]:co["cmean"] + 1], xb[:, k, :])
                    for k in range(8)], bank(0), reads=bregs(12288, 4096) + (env["r_consts"],))
        c.mm_group([(ps[0:1, 1, :], cbf[:, co["cmean"]:co["cmean"] + 1], xsq[:, k, :])
                    for k in range(8)], bank(1), reads=bregs(49152, 4096) + (env["r_consts"],))
        m_sb = SM[0:1, 64:576]
        var_sb = SM[0:1, 576:1088]
        rstd_sb = SM[0:1, 1088:1600]
        mb_bf = SMB[0:1, 0:512]
        rb_bf = SMB[0:1, 512:1024]
        c.step("vector", lambda e: e.tensor_copy(m_sb, ps[0:1, 0, :]), reads=(bank(0),), writes=(r_stats,))
        c.step("vector", lambda e: e.tensor_mul(var_sb, m_sb, m_sb), reads=(r_stats,), writes=(r_stats,))
        c.step("vector", lambda e: e.tensor_tensor(var_sb, ps[0:1, 1, :], var_sb, ALU.subtract),
               reads=(bank(1), r_stats), writes=(r_stats,))
        c.step("scalar", lambda e: e.activation(out=rstd_sb, in_=var_sb, func=AF.Sqrt,
                                                bias=cf32[0:1, cfo["eps_ln"]:cfo["eps_ln"] + 1]),
               reads=(r_stats,), writes=(r_stats,))
        c.step("vector", lambda e: e.reciprocal(rstd_sb, rstd_sb), reads=(r_stats,), writes=(r_stats,))
        c.step("vector", lambda e: e.tensor_copy(mb_bf, m_sb), reads=(r_stats,), writes=(r_stats,))
        c.step("vector", lambda e: e.tensor_copy(rb_bf, rstd_sb), reads=(r_stats,), writes=(r_stats,))
        c.mm_group([(ps[:, 0, :], cbf[0:1, co["ones_row"]:co["ones_row"] + 128], mb_bf)],
                   bank(0), reads=(r_stats, env["r_consts"]))
        c.mm_group([(ps[:, 1, :], cbf[0:1, co["ones_row"]:co["ones_row"] + 128], rb_bf)],
                   bank(1), reads=(r_stats, env["r_consts"]))
        T0 = A32[:, 20480:20992]
        for k in range(8):
            c.step("vector", lambda e, k=k: e.tensor_tensor(T0, xt[:, k, :], ps[:, 0, :], ALU.subtract),
                   reads=aregs(0, 4096) + (bank(0),), writes=aregs(20480, 512))
            c.step("vector", lambda e, k=k: e.tensor_tensor(lnf[:, k, :], T0, ps[:, 1, :], ALU.mult),
                   reads=aregs(20480, 512) + (bank(1),), writes=aregs(4096, 4096))
        for m in range(8):
            b = 2 + (m % 2)
            c.mm_group([(ps[:, b, :], AB[:, 0:8192].rearrange("p (k m j) -> p k m j", k=8, m=8)[:, k, m, :],
                         Cbt[:, k, :]) for k in range(8)],
                       bank(b), reads=bregs(8192, 4096) + bregs(0, 8192))
            c.step("vector", lambda e, m=m, b=b: e.scalar_tensor_tensor(
                T0, ps[:, b, :], RS, Yt[:, m, :], ALU.mult, ALU.add),
                reads=(bank(b),) + aregs(8192, 4096), writes=aregs(20480, 512))
            c.step("vector", lambda e, m=m: e.tensor_tensor(
                t1f[:, m, :], T0, lnf[:, m, :], ALU.add),
                reads=aregs(20480, 512) + aregs(4096, 4096), writes=aregs(12288, 4096))
        c.dma("sync", t1r[:, :, t0:t0 + TILE], t1f[:, :, :], reads=aregs(12288, 4096),
              writes=(r_t1,), sem=s["st"])
        t1b = AB[:, 12288:16384].rearrange("p (c t) -> p c t", c=8)
        c.step("gpsimd", lambda e: e.tensor_copy(t1b[:, :, :], t1f[:, :, :]),
               reads=aregs(12288, 4096), writes=bregs(12288, 4096))
        t1sq = AB[:, 49152:53248].rearrange("p (c t) -> p c t", c=8)
        c.step("scalar", lambda e: e.activation(out=t1sq[:, :, :], in_=t1f[:, :, :], func=AF.Square),
               reads=aregs(12288, 4096), writes=bregs(49152, 4096))
        c.mm_group([(ps[0:1, 0, :], cbf[:, co["cmean"]:co["cmean"] + 1], t1sq[:, k, :])
                    for k in range(8)], bank(0), reads=bregs(49152, 4096) + (env["r_consts"],))
        c.step("scalar", lambda e: e.activation(out=rstd_sb, in_=ps[0:1, 0, :], func=AF.Sqrt,
                                                bias=cf32[0:1, cfo["eps_rms6"]:cfo["eps_rms6"] + 1]),
               reads=(bank(0),), writes=(r_stats,))
        c.step("vector", lambda e: e.reciprocal(rstd_sb, rstd_sb), reads=(r_stats,), writes=(r_stats,))
        c.step("vector", lambda e: e.tensor_copy(rb_bf, rstd_sb), reads=(r_stats,), writes=(r_stats,))
        c.mm_group([(ps[:, 1, :], cbf[0:1, co["ones_row"]:co["ones_row"] + 128], rb_bf)],
                   bank(1), reads=(r_stats, env["r_consts"]))
        h2b = AB[:, 8192:12288].rearrange("p (c t) -> p c t", c=8)
        for k in range(8):
            c.step("vector", lambda e, k=k: e.tensor_tensor(T0, t1f[:, k, :], ps[:, 1, :], ALU.mult),
                   reads=aregs(12288, 4096) + (bank(1),), writes=aregs(20480, 512))
            c.step("vector", lambda e, k=k: e.tensor_scalar(
                h2b[:, k, :], T0, SM[:, 24 + k:25 + k], None, ALU.mult),
                reads=aregs(20480, 512) + (r_small,), writes=bregs(8192, 4096))
        sst = AB[:, 49152:53248].rearrange("p (g t) -> p g t", g=8)
        for mg in range(4):
            for m in range(8):
                mm_ = mg * 8 + m
                b = 4 + (m % 2)
                c.mm_group([(ps[:, b, :], wfcv[:, k, mm_, :], h2b[:, k, :]) for k in range(8)],
                           bank(b), reads=bregs(8192, 4096) + bregs(16384, 32768))
                c.step("scalar", lambda e, m=m, b=b: e.activation(
                    out=sst[:, m, :], in_=ps[:, b, :], func=AF.Silu),
                    reads=(bank(b),), writes=bregs(49152, 4096))
            c.dma("scalar", sfr[:, mg * 8:(mg + 1) * 8, t0:t0 + TILE], sst[:, :, :],
                  reads=bregs(49152, 4096), writes=(r_sfc,), sem=s["st"])

    c.dma("gpsimd", AB[:, 0:32768], wpj[:, :], writes=bregs(0, 32768))
    wpjv = AB[:, 0:32768].rearrange("p (k m j) -> p k m j", k=32, m=8)
    for n in range(NT if "p4" in phases else 0):
        t0 = n * TILE
        xt = A32[:, 0:4096].rearrange("p (c t) -> p c t", c=8)
        c.dma("sync", xt[:, :, :], xr[:, :, t0:t0 + TILE], reads=(r_xin,), writes=aregs(0, 4096))
        t1f = A32[:, 4096:8192].rearrange("p (c t) -> p c t", c=8)
        c.dma("sync", t1f[:, :, :], t1r[:, :, t0:t0 + TILE], reads=(r_t1,), writes=aregs(4096, 4096))
        sfA = AB[:, 32768:40960].rearrange("p (k t) -> p k t", k=16)
        sfB = AB[:, 40960:49152].rearrange("p (k t) -> p k t", k=16)
        c.dma("gpsimd", sfA[:, :, :], sfr[:, 0:16, t0:t0 + TILE], reads=(r_sfc,),
              writes=bregs(32768, 8192))
        c.dma("gpsimd", sfB[:, :, :], sfr[:, 16:32, t0:t0 + TILE], reads=(r_sfc,),
              writes=bregs(40960, 8192))
        T0 = A32[:, 20480:20992]
        for m in range(8):
            b = m % 2
            items = [(ps[:, b, :], wpjv[:, k, m, :], sfA[:, k, :]) for k in range(16)]
            items += [(ps[:, b, :], wpjv[:, 16 + k, m, :], sfB[:, k, :]) for k in range(16)]
            c.mm_group(items, bank(b), reads=bregs(0, 32768) + bregs(32768, 8192) + bregs(40960, 8192))
            c.step("vector", lambda e, m=m, b=b: e.scalar_tensor_tensor(
                T0, ps[:, b, :], RS, t1f[:, m, :], ALU.mult, ALU.add),
                reads=(bank(b),) + aregs(4096, 4096), writes=aregs(20480, 512))
            c.step("vector", lambda e, m=m: e.tensor_tensor(
                xt[:, m, :], xt[:, m, :], T0, ALU.add),
                reads=aregs(20480, 512) + aregs(0, 4096), writes=aregs(0, 4096))
        c.dma("sync", xor_[:, :, t0:t0 + TILE], xt[:, :, :], reads=aregs(0, 4096),
              writes=(r_xout,), sem=s["st"])


def emit_l1a(c: Ctx):
    s = c.s
    env = c.env
    ps = env["ps"]
    A32 = env["A32"]
    AB = env["AB"]
    SM = env["SM32"]
    SMB = env["SMB"]
    cbf = env["cbf"]
    co = env["cbf_off"]
    cf32 = env["cf32"]
    cfo = env["cf32_off"]
    R = env["R"]
    H, DK, DV, CH = 4, 256, 256, 32
    TE = T + 512
    NTE = TE // 512
    NC = T // CH

    def res(name):
        if name not in R:
            R[name] = Res()
        return R[name]

    def bank(i):
        return res(("bank", i))

    def aregs(start, width):
        return tuple(res(("A32blk", b)) for b in range(start // 4096, (start + width - 1) // 4096 + 1))

    def bregs(start, width):
        return tuple(res(("ABblk", b)) for b in range(start // 4096, (start + width - 1) // 4096 + 1))

    x_in = env["l1a_x_in"]
    x_out = env["l1a_x_out"]
    wqkvb = env["l1a_wqkvb"]
    wconv = env["l1a_wconv"]
    wfir = env["l1a_wfir"]
    wfu1 = env["l1a_wfu1"]
    wfu2 = env["l1a_wfu2"]
    wo = env["l1a_wo"]
    wnorm = env["l1a_wnorm"]
    wtau = env["l1a_wtau"]
    smask = env["smask"]
    prevr = env["prev"]
    pairlo = env["l1a_pairlo"]
    halfsel = env["l1a_halfsel"]

    lns = env["l1a_ln"]
    zq = env["l1a_zq"]
    zk = env["l1a_zk"]
    zv = env["l1a_zv"]
    qn_s = env["l1a_qn"]
    kn_s = env["l1a_kn"]
    kb_s = env["l1a_kb"]
    vb_s = env["l1a_vb"]
    v_s = env["l1a_v"]
    fs_s = env["l1a_fs"]
    fl_s = env["l1a_fl"]
    wfe_s = env["l1a_wfe"]
    utok_s = env["l1a_utok"]
    hcon = env["l1a_hcontrib"]
    hgat = env["l1a_hgather"]
    fcon = env["l1a_fcontrib"]
    fgat = env["l1a_fgather"]
    tcon = env["l1a_tcontrib"]
    tgat = env["l1a_tgather"]

    def fm(t_):
        return t_.rearrange("(c p) t -> p c t", p=128)

    xr = fm(x_in)
    xor_ = fm(x_out)
    lnr, zqr, zkr, zvr = fm(lns), fm(zq), fm(zk), fm(zv)
    qnr, knr, kbr, vbr, vr = fm(qn_s), fm(kn_s), fm(kb_s), fm(vb_s), fm(v_s)
    fsr, flr = fm(fs_s), fm(fl_s)
    wfer = fm(wfe_s)

    r_xin = res(("hbm", x_in.name))
    r_xout = res(("hbm", x_out.name))
    r_z = res("HBM_l1a_z")
    r_ln = res("HBM_l1a_ln")
    r_p1 = res("HBM_l1a_p1")
    r_fir = res("HBM_l1a_fir")
    r_p3a = res("HBM_l1a_p3a")
    r_o = res("HBM_l1a_o")
    r_coll = res("HBM_l1a_coll")
    r_stats = res("SM_STATS")
    r_small = res("SM_L1A")
    r_beta = res("SMB_BETA")

    PAIRS = [[0, 1], [2, 3], [4, 5], [6, 7]]
    c.dma("gpsimd", hcon.rearrange("(c p) t -> p c t", p=128),
          xr[:, :, T - 512:T], reads=(r_xin,), writes=(r_coll,), sem=s["w"])
    c.rd("gpsimd", r_coll)
    c.emit("gpsimd", lambda e: e.collective_compute(
        "AllGather", ALU.bypass, replica_groups=PAIRS,
        ins=[hcon.opt()], outs=[hgat.opt()]), inc=s["coll"], inc_n=1)
    c.did_wr(s["coll"], res("HBM_l1a_hgat"))

    c.dma("gpsimd", env["pt_i32"], prevr[:, :], writes=(res("PT"),))
    c.dma("gpsimd", env["pl_i32"], pairlo[:, :], writes=(res("PT"),))
    c.dma("gpsimd", env["hs_i32"], halfsel[:, :], writes=(res("PT"),))
    c.dma("sync", SM[:, 16:17], smask[:, :], writes=(r_small,))
    c.dma("sync", SM[:, 17:113], wconv[:, :], writes=(r_small,))
    c.dma("sync", SM[:, 113:385], wfir[:, :], writes=(r_small,))
    c.dma("sync", SM[:, 385:388], wnorm[:, :], writes=(r_small,))
    c.dma("scalar", env["tau_sb"], wtau[:, :], writes=(r_small,), sem=s["st"])

    c.dma("gpsimd", AB[:, 0:24608], wqkvb[:, :], writes=bregs(0, 24608))
    wq_v = AB[:, 0:8192].rearrange("p (k m j) -> p k m j", k=8, m=8)
    wk_v = AB[:, 8192:16384].rearrange("p (k m j) -> p k m j", k=8, m=8)
    wv_v = AB[:, 16384:24576].rearrange("p (k m j) -> p k m j", k=8, m=8)
    wb_v = AB[:, 24576:24608].rearrange("p (k m) -> p k m", k=8)
    beta_sb = A32[0:4, 16384:16384 + TE]
    rbeta = aregs(16384, TE)
    for n in range(NTE):
        t0 = n * 512
        xt = A32[:, 0:4096].rearrange("p (c t) -> p c t", c=8)
        if n == 0:
            c.dma("sync", xt[:, :, :],
                  hgat.rearrange("(r c p) t -> r p c t", r=2, p=128)[0, :, :, :],
                  reads=(res("HBM_l1a_hgat"),), writes=aregs(0, 4096))
            for k in range(8):
                c.step("vector", lambda e, k=k: e.tensor_scalar(
                    xt[:, k, :], xt[:, k, :], SM[:, 16:17], None, ALU.mult),
                    reads=aregs(0, 4096) + (r_small,), writes=aregs(0, 4096))
        else:
            c.dma("sync", xt[:, :, :], xr[:, :, t0 - 512:t0], reads=(r_xin,), writes=aregs(0, 4096))
        xb = AB[:, 24608:28704].rearrange("p (c t) -> p c t", c=8)
        c.step("gpsimd", lambda e: e.tensor_copy(xb[:, :, :], xt[:, :, :]),
               reads=aregs(0, 4096), writes=bregs(24608, 4096))
        xsq = AB[:, 28704:32800].rearrange("p (c t) -> p c t", c=8)
        c.step("scalar", lambda e: e.activation(out=xsq[:, :, :], in_=xt[:, :, :], func=AF.Square),
               reads=aregs(0, 4096), writes=bregs(28704, 4096))
        c.mm_group([(ps[0:1, 0, :], cbf[:, co["cmean"]:co["cmean"] + 1], xb[:, k, :])
                    for k in range(8)], bank(0), reads=bregs(24608, 4096) + (env["r_consts"],))
        c.mm_group([(ps[0:1, 1, :], cbf[:, co["cmean"]:co["cmean"] + 1], xsq[:, k, :])
                    for k in range(8)], bank(1), reads=bregs(28704, 4096) + (env["r_consts"],))
        m_sb = SM[0:1, 576:1088]
        var_sb = SM[0:1, 1088:1600]
        mb_bf = SMB[0:1, 0:512]
        rb_bf = SMB[0:1, 512:1024]
        c.step("vector", lambda e: e.tensor_copy(m_sb, ps[0:1, 0, :]), reads=(bank(0),), writes=(r_stats,))
        c.step("vector", lambda e: e.tensor_mul(var_sb, m_sb, m_sb), reads=(r_stats,), writes=(r_stats,))
        c.step("vector", lambda e: e.tensor_tensor(var_sb, ps[0:1, 1, :], var_sb, ALU.subtract),
               reads=(bank(1), r_stats), writes=(r_stats,))
        c.step("scalar", lambda e: e.activation(out=var_sb, in_=var_sb, func=AF.Sqrt,
                                                bias=cf32[0:1, cfo["eps_ln"]:cfo["eps_ln"] + 1]),
               reads=(r_stats,), writes=(r_stats,))
        c.step("vector", lambda e: e.reciprocal(var_sb, var_sb), reads=(r_stats,), writes=(r_stats,))
        c.step("vector", lambda e: e.tensor_copy(mb_bf, m_sb), reads=(r_stats,), writes=(r_stats,))
        c.step("vector", lambda e: e.tensor_copy(rb_bf, var_sb), reads=(r_stats,), writes=(r_stats,))
        c.mm_group([(ps[:, 0, :], cbf[0:1, co["ones_row"]:co["ones_row"] + 128], mb_bf)],
                   bank(0), reads=(r_stats, env["r_consts"]))
        c.mm_group([(ps[:, 1, :], cbf[0:1, co["ones_row"]:co["ones_row"] + 128], rb_bf)],
                   bank(1), reads=(r_stats, env["r_consts"]))
        lnb = AB[:, 24608:28704].rearrange("p (c t) -> p c t", c=8)
        T0 = A32[:, 20992:21504]
        for k in range(8):
            c.step("vector", lambda e, k=k: e.tensor_tensor(T0, xt[:, k, :], ps[:, 0, :], ALU.subtract),
                   reads=aregs(0, 4096) + (bank(0),), writes=aregs(20992, 512))
            c.step("vector", lambda e, k=k: e.tensor_tensor(lnb[:, k, :], T0, ps[:, 1, :], ALU.mult),
                   reads=aregs(20992, 512) + (bank(1),), writes=bregs(24608, 4096))
        c.dma("sync", lnr[:, :, t0:t0 + 512], lnb[:, :, :], reads=bregs(24608, 4096),
              writes=(r_ln,), sem=s["st"])
        zt = AB[:, 28704:32800].rearrange("p (c t) -> p c t", c=8)
        for which, wv, zr_ in (("q", wq_v, zqr), ("k", wk_v, zkr), ("v", wv_v, zvr)):
            for m in range(8):
                b = 2 + (m % 2)
                c.mm_group([(ps[:, b, :], wv[:, k, m, :], lnb[:, k, :]) for k in range(8)],
                           bank(b), reads=bregs(24608, 4096) + bregs(0, 24608))
                c.step("vector", lambda e, m=m, b=b: e.tensor_copy(zt[:, m, :], ps[:, b, :]),
                       reads=(bank(b),), writes=bregs(28704, 4096))
            c.dma("sync", zr_[:, :, t0:t0 + 512], zt[:, :, :], reads=bregs(28704, 4096),
                  writes=(r_z,), sem=s["st"])
        c.mm_group([(ps[0:4, 4, :], wb_v[:, k, :], lnb[:, k, :]) for k in range(8)],
                   bank(4), reads=bregs(24608, 4096) + bregs(0, 24608))
        c.step("scalar", lambda e, t0=t0: e.activation(
            out=beta_sb[:, t0:t0 + 512], in_=ps[0:4, 4, :], func=AF.Sigmoid),
            reads=(bank(4),), writes=rbeta)

    def convtap(ti, dc, j):
        return SM[:, 17 + ti * 32 + dc * 4 + j:17 + ti * 32 + dc * 4 + j + 1]

    for n in range(NTE):
        t0 = n * 512
        lo = max(t0 - 3, 0)
        w_in = t0 + 512 - lo
        for ti, zr_ in ((0, zqr), (1, zkr), (2, zvr)):
            zin = AB[:, ti * 4352:ti * 4352 + 4352].rearrange("p (c t) -> p c t", c=8)[:, :, 0:w_in]
            c.dma("sync", zin[:, :, :], zr_[:, :, lo:t0 + 512], reads=(r_z,),
                  writes=bregs(ti * 4352, 4352))
        qv = AB[:, 13056:17152].rearrange("p (c t) -> p c t", c=8)
        kv = AB[:, 17152:21248].rearrange("p (c t) -> p c t", c=8)
        vv = AB[:, 21248:25344].rearrange("p (c t) -> p c t", c=8)
        for ti, outv, outR in ((0, qv, bregs(13056, 4096)), (1, kv, bregs(17152, 4096)),
                               (2, vv, bregs(21248, 4096))):
            zin = AB[:, ti * 4352:ti * 4352 + 4352].rearrange("p (c t) -> p c t", c=8)
            zR = bregs(ti * 4352, 4352)
            T0 = A32[:, 20992:21504]
            for dc in range(8):
                off = t0 - lo
                for j in range(4):
                    sh = off - 3 + j
                    src = zin[:, dc, sh if sh >= 0 else 0:sh + 512 if sh + 512 <= w_in else w_in]
                    if sh < 0:
                        pass
                    if j == 0:
                        if sh < 0:
                            c.step("vector", lambda e: e.memset(T0, 0.0),
                                   reads=(), writes=aregs(20992, 512))
                            c.step("vector", lambda e, dc=dc, j=j, sh=sh: e.tensor_scalar(
                                T0[:, -sh:512], zin[:, dc, 0:512 + sh], convtap(ti, dc, j), None, ALU.mult),
                                reads=zR + (r_small,), writes=aregs(20992, 512))
                        else:
                            c.step("vector", lambda e, dc=dc, j=j, sh=sh: e.tensor_scalar(
                                T0, zin[:, dc, sh:sh + 512], convtap(ti, dc, j), None, ALU.mult),
                                reads=zR + (r_small,), writes=aregs(20992, 512))
                    else:
                        if sh < 0:
                            c.step("vector", lambda e, dc=dc, j=j, sh=sh: e.scalar_tensor_tensor(
                                T0[:, -sh:512], zin[:, dc, 0:512 + sh], convtap(ti, dc, j),
                                T0[:, -sh:512], ALU.mult, ALU.add),
                                reads=zR + (r_small,) + aregs(20992, 512), writes=aregs(20992, 512))
                        else:
                            c.step("vector", lambda e, dc=dc, j=j, sh=sh: e.scalar_tensor_tensor(
                                T0, zin[:, dc, sh:sh + 512], convtap(ti, dc, j),
                                T0, ALU.mult, ALU.add),
                                reads=zR + (r_small,) + aregs(20992, 512), writes=aregs(20992, 512))
                c.step("scalar", lambda e, dc=dc, outv=outv: e.activation(
                    out=outv[:, dc, :], in_=T0, func=AF.Silu),
                    reads=aregs(20992, 512), writes=outR)
        c.dma("scalar", vr[:, :, t0:t0 + 512], vv[:, :, :], reads=bregs(21248, 4096),
              writes=(r_p1,), sem=s["st"])
        rq = A32[:, 21504:22016]
        for src, dstr in ((qv, qnr), (kv, knr)):
            srcR = bregs(13056, 4096) if src is qv else bregs(17152, 4096)
            sq = AB[:, 25344:29440].rearrange("p (c t) -> p c t", c=8)
            c.step("scalar", lambda e, src=src, sq=sq: e.activation(
                out=sq[:, :, :], in_=src[:, :, :], func=AF.Square),
                reads=srcR, writes=bregs(25344, 4096))
            for h in range(4):
                b = 2 + (h % 2)
                c.mm_group([(ps[:, b, :], cbf[:, co["ones_row"]:co["ones_row"] + 128], sq[:, 2 * h + kk, :])
                            for kk in range(2)], bank(b), reads=bregs(25344, 4096) + (env["r_consts"],))
                c.step("scalar", lambda e, b=b: e.activation(
                    out=rq, in_=ps[:, b, :], func=AF.Sqrt,
                    bias=cf32[:, cfo["eps_rms6"]:cfo["eps_rms6"] + 1]),
                    reads=(bank(b),), writes=aregs(21504, 512))
                c.step("vector", lambda e: e.reciprocal(rq, rq),
                       reads=aregs(21504, 512), writes=aregs(21504, 512))
                for kk in range(2):
                    c.step("vector", lambda e, h=h, kk=kk, src=src: e.tensor_tensor(
                        src[:, 2 * h + kk, :], src[:, 2 * h + kk, :], rq, ALU.mult),
                        reads=aregs(21504, 512) + srcR, writes=srcR)
            c.dma("sync", dstr[:, :, t0:t0 + 512], src[:, :, :], reads=srcR,
                  writes=(r_p1,), sem=s["st"])
        kbv = AB[:, 25344:29440].rearrange("p (c t) -> p c t", c=8)
        vbv = AB[:, 29440:33536].rearrange("p (c t) -> p c t", c=8)
        bb = A32[:, 22016:22528]
        for h in range(4):
            sel = cbf[0:4, co["sel8"] + h * 128: co["sel8"] + (h + 1) * 128]
            BB2 = AB[0:4, 33536:34048]
            c.step("vector", lambda e, h=h, t0=t0, BB2=BB2: e.tensor_copy(
                BB2, beta_sb[:, t0:t0 + 512]),
                reads=rbeta, writes=bregs(33536, 512))
            c.mm_group([(ps[:, 6, :], sel, BB2)], bank(6),
                       reads=bregs(33536, 512) + (env["r_consts"],))
            c.step("vector", lambda e: e.tensor_copy(bb, ps[:, 6, :]),
                   reads=(bank(6),), writes=aregs(22016, 512))
            for kk in range(2):
                c.step("vector", lambda e, h=h, kk=kk: e.tensor_tensor(
                    kbv[:, 2 * h + kk, :], kv[:, 2 * h + kk, :], bb, ALU.mult),
                    reads=aregs(22016, 512) + bregs(17152, 4096), writes=bregs(25344, 4096))
                c.step("gpsimd", lambda e, h=h, kk=kk: e.tensor_tensor(
                    vbv[:, 2 * h + kk, :], vv[:, 2 * h + kk, :], bb, ALU.mult),
                    reads=aregs(22016, 512) + bregs(21248, 4096), writes=bregs(29440, 4096))
        c.dma("sync", kbr[:, :, t0:t0 + 512], kbv[:, :, :], reads=bregs(25344, 4096),
              writes=(r_p1,), sem=s["st"])
        c.dma("scalar", vbr[:, :, t0:t0 + 512], vbv[:, :, :], reads=bregs(29440, 4096),
              writes=(r_p1,), sem=s["st"])
    def firtap(dc, j):
        return SM[:, 113 + dc * 34 + j:113 + dc * 34 + j + 1]

    for n in range(NT):
        t0e = 512 + n * 512
        lo = t0e - 32
        vv = AB[:, 0:4352].rearrange("p (c t) -> p c t", c=8)
        c.dma("sync", vv[:, :, 0:544], vr[:, :, lo:t0e + 512], reads=(r_p1,),
              writes=bregs(0, 4352))
        fsv = AB[:, 4352:8448].rearrange("p (c t) -> p c t", c=8)
        flv = AB[:, 8448:12544].rearrange("p (c t) -> p c t", c=8)
        T0 = A32[:, 20480:20992]
        T1 = A32[:, 20992:21504]
        for dc in range(8):
            for j in range(3):
                sh = 30 + j
                if j == 0:
                    c.step("vector", lambda e, dc=dc, j=j, sh=sh: e.tensor_scalar(
                        T0, vv[:, dc, sh:sh + 512], firtap(dc, j), None, ALU.mult),
                        reads=bregs(0, 4352) + (r_small,), writes=aregs(20480, 512))
                else:
                    c.step("vector", lambda e, dc=dc, j=j, sh=sh: e.scalar_tensor_tensor(
                        T0, vv[:, dc, sh:sh + 512], firtap(dc, j), T0, ALU.mult, ALU.add),
                        reads=bregs(0, 4352) + (r_small,) + aregs(20480, 512),
                        writes=aregs(20480, 512))
            c.step("vector", lambda e, dc=dc: e.tensor_copy(fsv[:, dc, :], T0),
                   reads=aregs(20480, 512), writes=bregs(4352, 4096))
            for j in range(31):
                sh = 2 + j
                if j == 0:
                    c.step("vector", lambda e, dc=dc, j=j, sh=sh: e.tensor_scalar(
                        T1, vv[:, dc, sh:sh + 512], firtap(dc, 3 + j), None, ALU.mult),
                        reads=bregs(0, 4352) + (r_small,), writes=aregs(20992, 512))
                else:
                    c.step("vector", lambda e, dc=dc, j=j, sh=sh: e.scalar_tensor_tensor(
                        T1, vv[:, dc, sh:sh + 512], firtap(dc, 3 + j), T1, ALU.mult, ALU.add),
                        reads=bregs(0, 4352) + (r_small,) + aregs(20992, 512),
                        writes=aregs(20992, 512))
            c.step("vector", lambda e, dc=dc: e.tensor_copy(flv[:, dc, :], T1),
                   reads=aregs(20992, 512), writes=bregs(8448, 4096))
        c.dma("sync", fsr[:, :, n * 512:n * 512 + 512], fsv[:, :, :], reads=bregs(4352, 4096),
              writes=(r_fir,), sem=s["st"])
        c.dma("scalar", flr[:, :, n * 512:n * 512 + 512], flv[:, :, :], reads=bregs(8448, 4096),
              writes=(r_fir,), sem=s["st"])

    _mo = co["mask"]
    mlow = cbf[0:64, _mo:_mo + 32]
    mup = cbf[0:64, _mo + 32:_mo + 64]
    meye = cbf[0:64, _mo + 64:_mo + 96]
    R["MSK"] = env["r_consts"]
    for h in range(4):
        for g in range(T // 1024):
            tg = 512 + g * 1024
            for (reg0, srcs) in ((0, vb_s), (2048, kn_s), (8192, kb_s)):
                c.rd("sync", r_p1)
                c.wr("sync", *bregs(reg0, 2048))
                for j8 in range(8):
                    c.emit("sync", lambda e, h=h, tg=tg, reg0=reg0, srcs=srcs, j8=j8:
                           e.dma_start_transpose(
                               AB[:, reg0 + j8 * 256:reg0 + (j8 + 1) * 256],
                               srcs[h * 256:(h + 1) * 256, tg + j8 * 128:tg + (j8 + 1) * 128]),
                           inc=s["ld"], inc_n=16)
                c.did_wr(s["ld"], *bregs(reg0, 2048))
            vbt = AB[:, 0:2048].rearrange("p (r d) -> p r d", r=8)
            knt = AB[:, 2048:4096].rearrange("p (r d) -> p r d", r=8)
            kbt = AB[:, 8192:10240].rearrange("p (r d) -> p r d", r=8)
            c.dma("scalar", utok_s[g * 1024:(g + 1) * 1024, 1024 + h * 256:1024 + (h + 1) * 256]
                  .rearrange("(r p) d -> p r d", p=128),
                  knt[:, :, :], reads=bregs(2048, 2048), writes=(r_p3a,), sem=s["st"])
            kbf = AB[:, 4096:6144].rearrange("p (q t) -> p q t", q=2)
            knf = AB[:, 6144:8192].rearrange("p (q t) -> p q t", q=2)
            c.dma("sync", kbf[:, :, :], kbr[:, 2 * h:2 * h + 2, tg:tg + 1024], reads=(r_p1,),
                  writes=bregs(4096, 2048))
            c.dma("sync", knf[:, :, :], knr[:, 2 * h:2 * h + 2, tg:tg + 1024], reads=(r_p1,),
                  writes=bregs(6144, 2048))
            WSTG = AB[:, 10240:12288].rearrange("p (q t) -> p q t", q=2)
            rWS = bregs(10240, 2048)
            def bb2_body(bb2, inst):
                rA = (res(("p3aA", inst)),)
                rB = (res(("p3aB", inst)),)
                if ("_p3a_first", inst) not in R:
                    R[("_p3a_first", inst)] = True
                    xA = aregs(0, 4096)
                    xB = bregs(12288, 4096)
                else:
                    xA = ()
                    xB = ()
                tt = bb2 * 64
                rr0 = tt // 128
                pp0 = tt % 128
                ba = inst * 160
                bbs = 12288 + inst * 1152
                b0, b1, b2, b3 = 4 * inst, 4 * inst + 1, 4 * inst + 2, 4 * inst + 3
                AA = A32[0:64, ba:ba + 32]
                AT = A32[0:64, ba + 32:ba + 64]
                EE = A32[0:64, ba + 64:ba + 96]
                DD = A32[0:64, ba + 96:ba + 128]
                CC = A32[0:64, ba + 128:ba + 160]
                EB = AB[0:64, bbs:bbs + 32]
                DB = AB[0:64, bbs + 32:bbs + 64]
                CB = AB[0:64, bbs + 64:bbs + 96]
                VT = AB[0:64, bbs + 128:bbs + 384]
                KBT2 = AB[0:64, bbs + 384:bbs + 640]
                UST = AB[0:64, bbs + 640:bbs + 896]
                WST = AB[0:64, bbs + 896:bbs + 1152]
                for cj in range(2):
                    sl = slice(32 * cj, 32 * cj + 32)
                    c.mm_group([(ps[sl, b0, 0:32], kbf[:, kk, tt + 32 * cj:tt + 32 * cj + 32],
                                 knf[:, kk, tt + 32 * cj:tt + 32 * cj + 32]) for kk in range(2)],
                               bank(b0), reads=bregs(4096, 2048) + bregs(6144, 2048))
                    c.mm_group([(ps[sl, b1, 0:32], knf[:, kk, tt + 32 * cj:tt + 32 * cj + 32],
                                 kbf[:, kk, tt + 32 * cj:tt + 32 * cj + 32]) for kk in range(2)],
                               bank(b1), reads=bregs(4096, 2048) + bregs(6144, 2048))
                yield
                c.step("vector", lambda e: e.tensor_tensor(AA, ps[0:64, b0, 0:32], mlow, ALU.mult),
                       reads=(bank(b0), res("MSK")), writes=rA + xA)
                c.step("vector", lambda e: e.tensor_scalar(AA, AA, -1.0, None, ALU.mult),
                       reads=rA, writes=rA)
                c.step("vector", lambda e: e.tensor_tensor(AT, ps[0:64, b1, 0:32], mup, ALU.mult),
                       reads=(bank(b1), res("MSK")), writes=rA)
                c.step("vector", lambda e: e.tensor_scalar(AT, AT, -1.0, None, ALU.mult),
                       reads=rA, writes=rA)
                yield
                c.step("vector", lambda e: e.tensor_copy(EE, AA), reads=rA, writes=rA)
                c.step("vector", lambda e: e.tensor_copy(DD, AT), reads=rA, writes=rA)
                c.step("vector", lambda e: e.tensor_tensor(CC, AT, meye, ALU.add),
                       reads=rA + (res("MSK"),), writes=rA)
                c.step("vector", lambda e: e.tensor_copy(EB, EE), reads=rA, writes=rB + xB)
                c.step("vector", lambda e: e.tensor_copy(DB, DD), reads=rA, writes=rB)
                c.step("vector", lambda e: e.tensor_copy(CB, CC), reads=rA, writes=rB)
                yield
                for step_i in range(4):
                    for cj in range(2):
                        sl = slice(32 * cj, 32 * cj + 32)
                        c.mm_group([(ps[sl, b0, 0:32], EB[sl, :], DB[sl, :])], bank(b0), reads=rB)
                        c.mm_group([(ps[sl, b1, 0:32], DB[sl, :], EB[sl, :])], bank(b1), reads=rB)
                    yield
                    c.step("vector", lambda e: e.tensor_copy(DD, ps[0:64, b0, 0:32]),
                           reads=(bank(b0),), writes=rA)
                    c.step("vector", lambda e: e.tensor_copy(EE, ps[0:64, b1, 0:32]),
                           reads=(bank(b1),), writes=rA)
                    c.step("vector", lambda e: e.tensor_copy(DB, DD), reads=rA, writes=rB)
                    c.step("vector", lambda e: e.tensor_copy(EB, EE), reads=rA, writes=rB + xB)
                    yield
                    for cj in range(2):
                        sl = slice(32 * cj, 32 * cj + 32)
                        c.mm_group([(ps[sl, b2, 0:32], EB[sl, :], CB[sl, :])], bank(b2), reads=rB)
                    yield
                    c.step("vector", lambda e: e.tensor_tensor(CC, CC, ps[0:64, b2, 0:32], ALU.add),
                           reads=(bank(b2),) + rA, writes=rA)
                    c.step("vector", lambda e: e.tensor_copy(CB, CC), reads=rA, writes=rB)
                    yield
                c.step("vector", lambda e: e.tensor_copy(VT, vbt[pp0:pp0 + 64, rr0, :]),
                       reads=bregs(0, 2048), writes=rB)
                c.step("vector", lambda e: e.tensor_copy(KBT2, kbt[pp0:pp0 + 64, rr0, :]),
                       reads=bregs(8192, 2048), writes=rB)
                yield
                for cj in range(2):
                    sl = slice(32 * cj, 32 * cj + 32)
                    c.mm_group([(ps[sl, b2, 0:256], CB[sl, :], VT[sl, :])], bank(b2), reads=rB)
                    c.mm_group([(ps[sl, b3, 0:256], CB[sl, :], KBT2[sl, :])], bank(b3), reads=rB)
                yield
                c.step("vector", lambda e: e.tensor_copy(UST, ps[0:64, b2, 0:256]),
                       reads=(bank(b2),), writes=rB)
                c.step("vector", lambda e: e.tensor_copy(WST, ps[0:64, b3, 0:256]),
                       reads=(bank(b3),), writes=rB)
                c.dma("scalar", utok_s[g * 1024 + tt:g * 1024 + tt + 64, h * 256:(h + 1) * 256],
                      UST[:, :], reads=rB, writes=(r_p3a,), sem=s["st"])
                yield
                for q2 in range(2):
                    c.rd("tensor", *rB, env["r_consts"])
                    c.wr("tensor", bank(b3))
                    c.emit("tensor", lambda e, q2=q2, b3=b3, WST=WST: e.transpose(
                        ps[:, b3, 0:128].bitcast(BF16)[:, 0:64],
                        WST[:, q2 * 128:(q2 + 1) * 128],
                        cbf[0:64, co["idb"]:co["idb"] + 64]), inc=s["mm"])
                    c.did_rd(s["mm"], *rB)
                    c.did_wr(s["mm"], bank(b3))
                    c.step("vector", lambda e, q2=q2, tt=tt, b3=b3: e.tensor_copy(
                        WSTG[:, q2, tt:tt + 64], ps[:, b3, 0:128].bitcast(BF16)[:, 0:64]),
                        reads=(bank(b3),), writes=rWS)
                    yield

            for bp in range(8):
                gens = [bb2_body(2 * bp, 0), bb2_body(2 * bp + 1, 1)]
                alive = [True, True]
                while alive[0] or alive[1]:
                    for gi in range(2):
                        if alive[gi]:
                            try:
                                next(gens[gi])
                            except StopIteration:
                                alive[gi] = False
            c.dma("sync", wfer[:, 2 * h:2 * h + 2, tg - 512:tg - 512 + 1024], WSTG[:, :, :],
                  reads=rWS, writes=(r_p3a,), sem=s["st"])

    c.dma("sync", fcon[:, 0:T].rearrange("(c p) t -> p c t", p=128),
          qnr[:, :, 512:TE], reads=(r_p1,), writes=(r_coll,), sem=s["st"])
    c.dma("sync", fcon[:, T:2 * T].rearrange("(c p) t -> p c t", p=128),
          knr[:, :, 512:TE], reads=(r_p1,), writes=(r_coll,), sem=s["st"])
    c.dma("sync", fcon[:, 2 * T:3 * T].rearrange("(c p) t -> p c t", p=128),
          wfer[:, :, :], reads=(r_p3a,), writes=(r_coll,), sem=s["st"])
    c.dma("gpsimd", tcon[:, :], utok_s[:, :], reads=(r_p3a,), writes=(r_coll,), sem=s["w"])
    c.rd("gpsimd", r_coll)
    c.emit("gpsimd", lambda e: e.collective_compute(
        "AllGather", ALU.bypass, replica_groups=PAIRS,
        ins=[fcon.opt()], outs=[fgat.opt()]), inc=s["coll"], inc_n=1)
    c.emit("gpsimd", lambda e: e.collective_compute(
        "AllGather", ALU.bypass, replica_groups=PAIRS,
        ins=[tcon.opt()], outs=[tgat.opt()]), inc=s["coll"], inc_n=1)
    c.did_wr(s["coll"], res("HBM_l1a_gat"))

    emit_l1a_p3b(c, env)
    emit_l1a_p4(c, env)


def emit_l1a_p3b(c: Ctx, env):
    s = c.s
    ps = env["ps"]
    A32 = env["A32"]
    AB = env["AB"]
    cbf = env["cbf"]
    co = env["cbf_off"]
    R = env["R"]

    def res(name):
        if name not in R:
            R[name] = Res()
        return R[name]

    def bank(i):
        return res(("bank", i))

    def aregs(start, width):
        return tuple(res(("A32blk", b)) for b in range(start // 4096, (start + width - 1) // 4096 + 1))

    def bregs(start, width):
        return tuple(res(("ABblk", b)) for b in range(start // 4096, (start + width - 1) // 4096 + 1))

    fgat = env["l1a_fgather"]
    tgat = env["l1a_tgather"]
    fstg2 = env["l1a_fstg2"]
    tstg2 = env["l1a_tstg2"]
    ocon = env["l1a_ocon"]
    ogat = env["l1a_ogat"]
    oconr = ocon.rearrange("(c p) t -> p c t", p=128)
    r_o = res("HBM_l1a_o")
    mupinc = env["cbf"][0:32, env["cbf_off"]["mask"] + 96:env["cbf_off"]["mask"] + 128]

    Sf = [A32[:, h * 512:(h + 1) * 512].rearrange("p (q t) -> p q t", q=2) for h in range(2)]
    rS = aregs(0, 1024)
    Sb = [AB[:, 40960 + h * 512:40960 + (h + 1) * 512].rearrange("p (q t) -> p q t", q=2)
          for h in range(2)]
    rSb = bregs(40960, 1024)
    for h in range(2):
        c.step("gpsimd", lambda e, h=h: e.memset(Sf[h][:, :, :], 0.0),
               writes=rS + (res(("p3aA", 0)), res(("p3aA", 1))))
        c.step("vector", lambda e, h=h: e.tensor_copy(Sb[h][:, :, :], Sf[h][:, :, :]),
               reads=rS, writes=rSb)

    hbuf2 = [[AB[:, bu * 20480 + h * 10240:bu * 20480 + (h + 1) * 10240]
              for h in range(2)] for bu in range(2)]
    rHB2 = [[bregs(bu * 20480 + h * 10240, 10240) for h in range(2)] for bu in range(2)]
    stg = [AB[:, 43008 + h * 1280:43008 + (h + 1) * 1280] for h in range(2)]
    rSt = bregs(43008, 2560)
    r_gat = res("HBM_l1a_gat")
    r_stg2 = res("HBM_l1a_stg2")

    cell = env.setdefault("_l1a_hs", {})
    for hf in range(2):
        c.rd("gpsimd", r_gat)
        c.rd("gpsimd", res("PT"))
        c.wr("gpsimd", r_stg2)

        def _cpf(e, hf=hf, cell=cell):
            if "r" not in cell:
                cell["r"] = e.value_load(env["hs_i32"][0:1, 0:1])
            return e.dma_start(out=fstg2[hf, :, :],
                               in_=fgat[bass.ds(hf * 1024 + cell["r"] * 512, 512), :])
        c.emit("gpsimd", _cpf, inc=s["w"], inc_n=16)

        def _cpu(e, hf=hf, cell=cell):
            return e.dma_start(out=tstg2[hf, :, 0:512],
                               in_=tgat[hf * T:(hf + 1) * T, bass.ds(cell["r"] * 512, 512)])
        c.emit("gpsimd", _cpu, inc=s["w"], inc_n=16)

        def _cpk(e, hf=hf, cell=cell):
            return e.dma_start(out=tstg2[hf, :, 512:1024],
                               in_=tgat[hf * T:(hf + 1) * T, bass.ds(1024 + cell["r"] * 512, 512)])
        c.emit("gpsimd", _cpk, inc=s["w"], inc_n=16)
        c.did_wr(s["w"], r_stg2)

    fgvs = fstg2.rearrange("hf (rw p) t -> hf rw p t", p=128)
    tgvs = tstg2.rearrange("hf (rw p) d -> hf rw p d", p=128)
    for hf in range(2):
        for grp in range(T // 1024):
            tg = grp * 1024
            bu = (hf * 4 + grp) % 2
            hbuf = hbuf2[bu]
            rHB = rHB2[bu]
            for h in range(2):
                qf = hbuf[h][:, 0:2048].rearrange("p (q t) -> p q t", q=2)
                kf = hbuf[h][:, 2048:4096].rearrange("p (q t) -> p q t", q=2)
                wf = hbuf[h][:, 4096:6144].rearrange("p (q t) -> p q t", q=2)
                ut = hbuf[h][:, 6144:8192].rearrange("p (r d) -> p r d", r=8)
                kt = hbuf[h][:, 8192:10240].rearrange("p (r d) -> p r d", r=8)
                hb_extra = ((res(("p3aB", 0)), res(("p3aB", 1)))
                            if (hf == 0 and grp <= 1 and h == 0) else ())
                for (dst, coff) in ((qf, 0), (kf, T), (wf, 2 * T)):
                    c.dma("sync", dst[:, :, :],
                          fgvs[hf, 2 * h:2 * h + 2, :,
                               coff + tg:coff + tg + 1024].transpose([1, 0, 2]),
                          reads=(r_stg2,), writes=rHB[h] + hb_extra)
                for (dst, coff) in ((ut, 0), (kt, 512)):
                    c.dma("sync", dst[:, :, :],
                          tgvs[hf, tg // 128:tg // 128 + 8, :,
                               coff + h * 256:coff + (h + 1) * 256].transpose([1, 0, 2]),
                          reads=(r_stg2,), writes=rHB[h])

            HV = []
            for h in range(2):
                HV.append((
                    hbuf[h][:, 0:2048].rearrange("p (q t) -> p q t", q=2),
                    hbuf[h][:, 2048:4096].rearrange("p (q t) -> p q t", q=2),
                    hbuf[h][:, 4096:6144].rearrange("p (q t) -> p q t", q=2),
                    hbuf[h][:, 6144:8192].rearrange("p (r d) -> p r d", r=8),
                    hbuf[h][:, 8192:10240].rearrange("p (r d) -> p r d", r=8),
                    stg[h][0:64, 0:256],
                    stg[h][0:64, 256:512],
                    stg[h][0:64, 512:768],
                    stg[h][0:64, 768:800],
                    stg[h][0:64, 1024:1280],
                ))
            for cc2 in range(16):
                tt = cc2 * 64
                rr0 = tt // 128
                pp0 = tt % 128
                for h in range(2):
                    ut, kt, UT2, KT2 = HV[h][3], HV[h][4], HV[h][5], HV[h][6]
                    c.step("gpsimd", lambda e, UT2=UT2, ut=ut, pp0=pp0, rr0=rr0: e.tensor_copy(
                        UT2, ut[pp0:pp0 + 64, rr0, :]), reads=rHB[h], writes=rSt)
                    c.step("gpsimd", lambda e, KT2=KT2, kt=kt, pp0=pp0, rr0=rr0: e.tensor_copy(
                        KT2, kt[pp0:pp0 + 64, rr0, :]), reads=rHB[h], writes=rSt)
                for cj in range(2):
                    tc = tt + 32 * cj
                    for h in range(2):
                        qf, kf = HV[h][0], HV[h][1]
                        c.mm_group([(ps[0:32, 4 + h, 0:32], kf[:, q, tc:tc + 32],
                                     qf[:, q, tc:tc + 32]) for q in range(2)],
                                   bank(4 + h), reads=rHB[h])
                    for h in range(2):
                        LT = HV[h][8]
                        c.step("vector", lambda e, LT=LT, cj=cj, h=h: e.tensor_tensor(
                            LT[32 * cj:32 * cj + 32, :], ps[0:32, 4 + h, 0:32], mupinc, ALU.mult),
                            reads=(bank(4 + h), res("MSK")), writes=rSt)
                    for h in range(2):
                        wf = HV[h][2]
                        c.mm_group([(ps[0:32, h, 0:256], wf[:, q, tc:tc + 32], Sb[h][:, q, :])
                                    for q in range(2)], bank(h), reads=rHB[h] + (rSb[0],))
                    for h in range(2):
                        UT2, UADJ = HV[h][5], HV[h][7]
                        c.step("vector", lambda e, cj=cj, UT2=UT2, UADJ=UADJ, h=h: e.tensor_tensor(
                            UADJ[32 * cj:32 * cj + 32, :], UT2[32 * cj:32 * cj + 32, :],
                            ps[0:32, h, 0:256], ALU.subtract),
                            reads=(bank(h),) + rSt, writes=rSt)
                    for h in range(2):
                        qf, UADJ, LT = HV[h][0], HV[h][7], HV[h][8]
                        c.rd("tensor", *(rHB[h] + (rSb[0], rSt[0])))
                        c.wr("tensor", bank(h))
                        for q in range(2):
                            c.emit("tensor", lambda e, q=q, tc=tc, qf=qf, h=h: e.matmul(
                                ps[0:32, h, 0:256], qf[:, q, tc:tc + 32], Sb[h][:, q, :],
                                start=(q == 0), stop=False))
                        c.emit("tensor", lambda e, LT=LT, cj=cj, UADJ=UADJ, h=h: e.matmul(
                            ps[0:32, h, 0:256], LT[32 * cj:32 * cj + 32, :],
                            UADJ[32 * cj:32 * cj + 32, :],
                            start=False, stop=True), inc=s["mm"])
                        c.did_rd(s["mm"], *(rHB[h] + (rSb[0], rSt[0])))
                        c.did_wr(s["mm"], bank(h))
                    for h in range(2):
                        OST2 = HV[h][9]
                        c.step("vector", lambda e, cj=cj, OST2=OST2, h=h: e.tensor_copy(
                            OST2[32 * cj:32 * cj + 32, :], ps[0:32, h, 0:256]),
                            reads=(bank(h),), writes=rSt)
                    for q in range(2):
                        for h in range(2):
                            KT2, UADJ = HV[h][6], HV[h][7]
                            c.mm_group([(ps[:, 4 + h, 0:256],
                                         KT2[32 * cj:32 * cj + 32, q * 128:(q + 1) * 128],
                                         UADJ[32 * cj:32 * cj + 32, :])], bank(4 + h),
                                       reads=rSt)
                        for h in range(2):
                            c.step("vector", lambda e, q=q, h=h: e.tensor_tensor(
                                Sf[h][:, q, :], Sf[h][:, q, :], ps[:, 4 + h, 0:256], ALU.add),
                                reads=(bank(4 + h),) + rS, writes=rS)
                        for h in range(2):
                            c.step("scalar", lambda e, q=q, h=h: e.activation(
                                out=Sb[h][:, q, :], in_=Sf[h][:, q, :], func=AF.Identity),
                                reads=rS, writes=rSb)
                for q2 in range(2):
                    for h in range(2):
                        OST2 = HV[h][9]
                        c.rd("tensor", *rSt, env["r_consts"])
                        c.wr("tensor", bank(4 + h))
                        c.emit("tensor", lambda e, q2=q2, OST2=OST2, h=h: e.transpose(
                            ps[:, 4 + h, 0:128].bitcast(BF16)[:, 0:64],
                            OST2[:, q2 * 128:(q2 + 1) * 128],
                            cbf[0:64, co["idb"]:co["idb"] + 64]), inc=s["mm"])
                        c.did_rd(s["mm"], rSt[0])
                        c.did_wr(s["mm"], bank(4 + h))
                    for h in range(2):
                        OT = stg[h][:, 800 + q2 * 64:800 + (q2 + 1) * 64]
                        c.step("vector", lambda e, OT=OT, h=h: e.tensor_copy(
                            OT, ps[:, 4 + h, 0:128].bitcast(BF16)[:, 0:64]),
                            reads=(bank(4 + h),), writes=rSt)
                for h in range(2):
                    c.dma("scalar", oconr[:, 2 * h:2 * h + 2, hf * T + tg + tt:hf * T + tg + tt + 64],
                          stg[h][:, 800:928].rearrange("p (q t) -> p q t", q=2),
                          reads=rSt, writes=(r_o,), sem=s["st"])

    PAIRS = [[0, 1], [2, 3], [4, 5], [6, 7]]
    c.rd("gpsimd", r_o)
    c.emit("gpsimd", lambda e: e.collective_compute(
        "AllGather", ALU.bypass, replica_groups=PAIRS,
        ins=[ocon.opt()], outs=[ogat.opt()]), inc=s["coll"], inc_n=1)
    c.did_wr(s["coll"], res("HBM_l1a_ogat"))


def emit_l1a_p4(c: Ctx, env):
    s = c.s
    ps = env["ps"]
    A32 = env["A32"]
    AB = env["AB"]
    SM = env["SM32"]
    SMB = env["SMB"]
    cbf = env["cbf"]
    co = env["cbf_off"]
    cf32 = env["cf32"]
    cfo = env["cf32_off"]
    R = env["R"]

    def res(name):
        if name not in R:
            R[name] = Res()
        return R[name]

    def bank(i):
        return res(("bank", i))

    def aregs(start, width):
        return tuple(res(("A32blk", b)) for b in range(start // 4096, (start + width - 1) // 4096 + 1))

    def bregs(start, width):
        return tuple(res(("ABblk", b)) for b in range(start // 4096, (start + width - 1) // 4096 + 1))

    x_in = env["l1a_x_in"]
    x_out = env["l1a_x_out"]
    xr = x_in.rearrange("(c p) t -> p c t", p=128)
    xor_ = x_out.rearrange("(c p) t -> p c t", p=128)
    lnr = env["l1a_ln"].rearrange("(c p) t -> p c t", p=128)
    fsr = env["l1a_fs"].rearrange("(c p) t -> p c t", p=128)
    flr = env["l1a_fl"].rearrange("(c p) t -> p c t", p=128)
    vr = env["l1a_v"].rearrange("(c p) t -> p c t", p=128)
    wfu1 = env["l1a_wfu1"]
    wfu1b = env["l1a_wfu1b"]
    wfu2 = env["l1a_wfu2"]
    wo = env["l1a_wo"]
    r_xin = res(("hbm", x_in.name))
    r_xout = res(("hbm", x_out.name))
    r_o = res("HBM_l1a_o")
    r_fir = res("HBM_l1a_fir")
    r_ln = res("HBM_l1a_ln")
    r_p1 = res("HBM_l1a_p1")
    r_small = res("SM_L1A")
    r_stats = res("SM_STATS")

    ostg = env["l1a_ostg"]
    ostgr = ostg.rearrange("(c p) t -> p c t", p=128)
    ofed = env["l1a_ogat"]
    c.rd("gpsimd", res("HBM_l1a_ogat"))
    c.rd("gpsimd", res("PT"))
    c.wr("gpsimd", res("HBM_l1a_ostg"))
    def _cpo(e):
        r = e.value_load(env["hs_i32"][0:1, 0:1])
        return e.dma_start(out=ostg[:, :], in_=ofed[:, bass.ds(r * T, T)])
    c.emit("gpsimd", _cpo, inc=s["w"], inc_n=16)
    c.did_wr(s["w"], res("HBM_l1a_ostg"))
    c.dma("gpsimd", AB[:, 24576:24832], wfu2[:, :], writes=bregs(24576, 256))
    wfu2v = AB[:, 24576:24832].rearrange("p (k m) -> p k m", k=16)
    bfu2c = env["l1a_bfu2"]
    c.dma("sync", SM[:, 388:389], bfu2c[:, :], writes=(r_small,))
    c.dma("sync", SM[:, 389:405], wfu1b[:, :], writes=(r_small,))
    c.dma("gpsimd", AB[:, 16384:24576], wo[:, :], writes=bregs(16384, 8192))
    wov = AB[:, 16384:24576].rearrange("p (k m j) -> p k m j", k=8, m=8)
    c.step("scalar", lambda e: e.activation(out=env["tau_sb"], in_=env["tau_sb"],
                                            func=AF.Exp, scale=-1.0),
           reads=(r_small,), writes=(r_small,))
    c.step("vector", lambda e: e.tensor_copy(SMB[0:1, 0:4], env["tau_sb"][0:1, 0:4]),
           reads=(r_small,), writes=(r_small, r_stats))
    c.mm_group([(ps[:, 6, 0:4], cbf[0:1, co["ones_row"]:co["ones_row"] + 128], SMB[0:1, 0:4])],
               bank(6), reads=(r_small, env["r_consts"]))
    RT = A32[:, 20480:20484]
    c.step("vector", lambda e: e.tensor_copy(RT, ps[:, 6, 0:4]), reads=(bank(6),),
           writes=aregs(20480, 512))

    r_fslot = [res(("FU1", i)) for i in range(2)]
    for sup in range(T // 1024):
        t0 = sup * 1024
        hid = A32[:, 0:16384].rearrange("p (m t) -> p m t", m=16)
        rHID = aregs(0, 16384)
        for qty in range(4):
            gi = AB[:, 24832:33024].rearrange("p (k t) -> p k t", k=8)
            if qty == 0:
                c.dma("sync", gi[:, :, :], lnr[:, :, 512 + t0:512 + t0 + 1024],
                      reads=(r_ln,), writes=bregs(24832, 8192))
            elif qty == 1:
                c.dma("sync", gi[:, :, :], fsr[:, :, t0:t0 + 1024], reads=(r_fir,),
                      writes=bregs(24832, 8192))
            elif qty == 2:
                c.dma("sync", gi[:, :, :], flr[:, :, t0:t0 + 1024], reads=(r_fir,),
                      writes=bregs(24832, 8192))
            else:
                c.dma("sync", gi[:, :, :], ostgr[:, :, t0:t0 + 1024],
                      reads=(res("HBM_l1a_ostg"),), writes=bregs(24832, 8192))
            wsrc = wfu1[qty, :, :].rearrange("p (k m j) -> p k m j", k=8, m=16)
            for mh in range(2):
                slot = (qty * 2 + mh) % 2
                wb = AB[:, slot * 8192:(slot + 1) * 8192]
                rw = r_fslot[slot]
                wr_extra = bregs(0, 16384) if (sup == 0 and qty == 0) else ()
                c.dma("gpsimd", wb.rearrange("p (k m j) -> p k m j", k=8, m=8),
                      wsrc[:, :, mh * 8:(mh + 1) * 8, :], writes=(rw,) + tuple(wr_extra))
                wv = wb.rearrange("p (k m j) -> p k m j", k=8, m=8)
                for ml in range(8):
                    m = mh * 8 + ml
                    for n2 in range(2):
                        b = n2
                        c.mm_group([(ps[:, b, :], wv[:, k, ml, :], gi[:, k, n2 * 512:(n2 + 1) * 512])
                                    for k in range(8)], bank(b), reads=(rw,) + bregs(24832, 8192))
                        if qty == 0:
                            c.step("vector", lambda e, m=m, n2=n2, b=b: e.tensor_copy(
                                hid[:, m, n2 * 512:(n2 + 1) * 512], ps[:, b, :]),
                                reads=(bank(b),), writes=rHID)
                        else:
                            c.step("vector", lambda e, m=m, n2=n2, b=b: e.tensor_tensor(
                                hid[:, m, n2 * 512:(n2 + 1) * 512], ps[:, b, :],
                                hid[:, m, n2 * 512:(n2 + 1) * 512], ALU.add),
                                reads=(bank(b),) + rHID, writes=rHID)
        hidb = AB[:, 33024:49408].rearrange("p (m t) -> p m t", m=16)
        for m in range(16):
            c.step("scalar", lambda e, m=m: e.activation(
                out=hidb[:, m, :], in_=hid[:, m, :], func=AF.Gelu,
                bias=SM[:, 389 + m:390 + m]), reads=rHID + (r_small,),
                writes=bregs(33024, 16384))
        lg = A32[0:16, 16384:17408]
        for n2 in range(2):
            c.mm_group([(ps[0:16, 2 + n2, :], wfu2v[:, k, :], hidb[:, k, n2 * 512:(n2 + 1) * 512])
                        for k in range(16)], bank(2 + n2), reads=bregs(33024, 16384) + bregs(24576, 256))
            c.step("vector", lambda e, n2=n2: e.tensor_scalar(
                lg[:, n2 * 512:(n2 + 1) * 512], ps[0:16, 2 + n2, :], SM[0:16, 388:389],
                None, ALU.add), reads=(bank(2 + n2), r_small), writes=aregs(16384, 4096))
        W4 = A32[0:16, 17408:18432]
        for j in range(8):
            c.rd("tensor", aregs(16384, 4096)[0], env["r_consts2"])
            c.wr("tensor", bank(4))
            c.emit("tensor", lambda e, j=j: e.transpose(
                ps[:, 4, 0:16], lg[:, j * 128:(j + 1) * 128],
                cf32[0:16, cfo["id"]:cfo["id"] + 16]), inc=s["mm"])
            c.did_rd(s["mm"], aregs(16384, 4096)[0])
            c.did_wr(s["mm"], bank(4))
            G = A32[:, 18432:18448].rearrange("p (h j2) -> p h j2", h=4)
            pr = A32[:, 18448:18452]
            pv = A32[:, 18452:18456]
            oth = A32[:, 18456:18468].rearrange("p (h j2) -> p h j2", h=4)
            mx = A32[:, 18468:18472]
            sm_ = A32[:, 18472:18476]
            rG = aregs(18432, 512)
            c.step("vector", lambda e, G=G: e.tensor_copy(
                G[:, :, :], ps[:, 4, 0:16].rearrange("p (h j2) -> p h j2", h=4)),
                reads=(bank(4),), writes=rG)
            c.step("scalar", lambda e: e.activation(out=pr, in_=G[:, :, 3], func=AF.Sigmoid),
                   reads=rG, writes=rG)
            c.step("vector", lambda e: e.tensor_scalar(pv, pr, 0.09, 0.9, ALU.mult, ALU.add),
                   reads=rG, writes=rG)
            c.step("vector", lambda e: e.tensor_tensor(pv, pv, pr, ALU.mult),
                   reads=rG, writes=rG)
            c.step("vector", lambda e: e.tensor_reduce(mx, G[:, :, 0:3], AX.X, ALU.max),
                   reads=rG, writes=rG)
            c.step("vector", lambda e: e.tensor_tensor(
                oth[:, :, :], G[:, :, 0:3], mx.unsqueeze(2).broadcast_to([128, 4, 3]),
                ALU.subtract), reads=rG, writes=rG)
            c.step("vector", lambda e: e.tensor_tensor(
                oth[:, :, :], oth[:, :, :], RT.unsqueeze(2).broadcast_to([128, 4, 3]),
                ALU.mult), reads=rG + aregs(20480, 512), writes=rG)
            c.step("scalar", lambda e: e.activation(out=oth[:, :, :], in_=oth[:, :, :], func=AF.Exp),
                   reads=rG, writes=rG)
            c.step("vector", lambda e: e.tensor_reduce(sm_, oth[:, :, :], AX.X, ALU.add),
                   reads=rG, writes=rG)
            c.step("vector", lambda e: e.reciprocal(sm_, sm_), reads=rG, writes=rG)
            c.step("vector", lambda e: e.tensor_tensor(
                oth[:, :, :], oth[:, :, :], sm_.unsqueeze(2).broadcast_to([128, 4, 3]), ALU.mult),
                reads=rG, writes=rG)
            c.step("vector", lambda e: e.tensor_scalar(mx, pv, -1.0, 1.0, ALU.mult, ALU.add),
                   reads=rG, writes=rG)
            W4t = A32[:, 18476:18492].rearrange("p (h j2) -> p h j2", h=4)
            c.step("vector", lambda e, W4t=W4t: e.tensor_tensor(
                W4t[:, :, 0:3], oth[:, :, :], mx.unsqueeze(2).broadcast_to([128, 4, 3]), ALU.mult),
                reads=rG, writes=rG)
            c.step("vector", lambda e, W4t=W4t: e.tensor_copy(W4t[:, :, 3], pv),
                   reads=rG, writes=rG)
            c.rd("tensor", rG[0], env["r_consts2"])
            c.wr("tensor", bank(4))
            c.emit("tensor", lambda e, W4t=W4t: e.transpose(
                ps[0:16, 4, 0:128], W4t.rearrange("p h j2 -> p (h j2)"),
                cf32[:, cfo["id"]:cfo["id"] + 128]), inc=s["mm"])
            c.did_rd(s["mm"], rG[0])
            c.did_wr(s["mm"], bank(4))
            c.step("vector", lambda e, j=j: e.tensor_copy(W4[:, j * 128:(j + 1) * 128],
                                                          ps[0:16, 4, 0:128]),
                   reads=(bank(4),), writes=aregs(17408, 1024))
        W4b = SMB[0:16, 0:1024]
        c.step("vector", lambda e: e.tensor_copy(W4b, W4), reads=aregs(17408, 1024),
               writes=(res("SMB_W4"), r_stats))
        fsv = AB[:, 24832:26880].rearrange("p (k t) -> p k t", k=2)
        flv = AB[:, 26880:28928].rearrange("p (k t) -> p k t", k=2)
        dv_ = AB[:, 28928:30976].rearrange("p (k t) -> p k t", k=2)
        vv = AB[:, 30976:33024].rearrange("p (k t) -> p k t", k=2)
        onb = AB[:, 33024:41216].rearrange("p (k t) -> p k t", k=8)
        T0 = A32[:, 20992:21504]
        T1 = A32[:, 21504:22016]
        for h in range(4):
            c.dma("sync", fsv[:, :, :], fsr[:, 2 * h:2 * h + 2, t0:t0 + 1024],
                  reads=(r_fir,), writes=bregs(24832, 2048))
            c.dma("sync", flv[:, :, :], flr[:, 2 * h:2 * h + 2, t0:t0 + 1024],
                  reads=(r_fir,), writes=bregs(26880, 2048))
            c.dma("sync", dv_[:, :, :], ostgr[:, 2 * h:2 * h + 2, t0:t0 + 1024],
                  reads=(res("HBM_l1a_ostg"),), writes=bregs(28928, 2048))
            c.dma("sync", vv[:, :, :], vr[:, 2 * h:2 * h + 2, 512 + t0:512 + t0 + 1024],
                  reads=(r_p1,), writes=bregs(30976, 2048))
            for n2 in range(2):
                BC = A32[:, 18944 + 0:18944 + 2048].rearrange("p (r t) -> p r t", r=4)
                for r4 in range(4):
                    sel = cbf[0:16, co["sel16"] + (4 * h + r4) * 128: co["sel16"] + (4 * h + r4 + 1) * 128]
                    c.mm_group([(ps[:, 5, :], sel, W4b[:, n2 * 512:(n2 + 1) * 512])],
                               bank(5), reads=(res("SMB_W4"), env["r_consts"]))
                    c.step("vector", lambda e, r4=r4, BC=BC: e.tensor_copy(BC[:, r4, :], ps[:, 5, :]),
                           reads=(bank(5),), writes=aregs(18944, 2048))
                for kk in range(2):
                    dc = 2 * h + kk
                    tsl = slice(n2 * 512, (n2 + 1) * 512)
                    c.step("vector", lambda e, kk=kk, tsl=tsl, BC=BC: e.tensor_tensor(
                        T0, fsv[:, kk, tsl], BC[:, 0, :], ALU.mult),
                        reads=bregs(24832, 2048) + aregs(18944, 2048), writes=aregs(20992, 512))
                    c.step("vector", lambda e, kk=kk, tsl=tsl, BC=BC: e.tensor_tensor(
                        T1, flv[:, kk, tsl], BC[:, 1, :], ALU.mult),
                        reads=bregs(26880, 2048) + aregs(18944, 2048), writes=aregs(21504, 512))
                    c.step("vector", lambda e: e.tensor_tensor(T0, T0, T1, ALU.add),
                           reads=aregs(20992, 512) + aregs(21504, 512), writes=aregs(20992, 512))
                    c.step("vector", lambda e, kk=kk, tsl=tsl, BC=BC: e.tensor_tensor(
                        T1, dv_[:, kk, tsl], BC[:, 2, :], ALU.mult),
                        reads=bregs(28928, 2048) + aregs(18944, 2048), writes=aregs(21504, 512))
                    c.step("vector", lambda e: e.tensor_tensor(T0, T0, T1, ALU.add),
                           reads=aregs(20992, 512) + aregs(21504, 512), writes=aregs(20992, 512))
                    c.step("vector", lambda e, kk=kk, tsl=tsl, BC=BC: e.tensor_tensor(
                        T1, vv[:, kk, tsl], BC[:, 3, :], ALU.mult),
                        reads=bregs(30976, 2048) + aregs(18944, 2048), writes=aregs(21504, 512))
                    c.step("vector", lambda e, dc=dc, tsl=tsl: e.tensor_tensor(
                        onb[:, dc, tsl], T0, T1, ALU.add),
                        reads=aregs(20992, 512) + aregs(21504, 512), writes=bregs(33024, 8192))
        osq = AB[:, 41216:49408].rearrange("p (k t) -> p k t", k=8)
        c.step("scalar", lambda e: e.activation(out=osq[:, :, :], in_=onb[:, :, :], func=AF.Square),
               reads=bregs(33024, 8192), writes=bregs(41216, 8192))
        rq = A32[:, 20992:21504]
        for h in range(4):
            for n2 in range(2):
                b = 6 + (n2 % 2)
                c.mm_group([(ps[:, b, :], cbf[:, co["ones_row"]:co["ones_row"] + 128],
                             osq[:, 2 * h + kk, n2 * 512:(n2 + 1) * 512]) for kk in range(2)],
                           bank(b), reads=bregs(41216, 8192) + (env["r_consts"],))
                c.step("scalar", lambda e, b=b: e.activation(
                    out=rq, in_=ps[:, b, :], func=AF.Sqrt, scale=1.0 / 256.0,
                    bias=cf32[:, cfo["eps_ln"]:cfo["eps_ln"] + 1]),
                    reads=(bank(b),), writes=aregs(20992, 512))
                c.step("vector", lambda e: e.reciprocal(rq, rq), reads=aregs(20992, 512),
                       writes=aregs(20992, 512))
                for kk in range(2):
                    dc = 2 * h + kk
                    tsl = slice(n2 * 512, (n2 + 1) * 512)
                    c.step("vector", lambda e, dc=dc, tsl=tsl: e.tensor_tensor(
                        onb[:, dc, tsl], onb[:, dc, tsl], rq, ALU.mult),
                        reads=aregs(20992, 512) + bregs(49408, 2048), writes=bregs(33024, 8192))
                    c.step("vector", lambda e, dc=dc, tsl=tsl, kk=kk: e.tensor_scalar(
                        onb[:, dc, tsl], onb[:, dc, tsl], SM[:, 385 + kk:386 + kk], None,
                        ALU.mult), reads=(r_small,) + bregs(49408, 2048), writes=bregs(33024, 8192))
        xt = A32[:, 0:8192].rearrange("p (c t) -> p c t", c=8)
        c.dma("sync", xt[:, :, :], xr[:, :, t0:t0 + 1024],
              reads=(r_xin,), writes=aregs(0, 8192))
        for m in range(8):
            for n2 in range(2):
                b = n2
                c.mm_group([(ps[:, b, :], wov[:, k, m, :], onb[:, k, n2 * 512:(n2 + 1) * 512])
                            for k in range(8)], bank(b), reads=bregs(33024, 8192) + bregs(16384, 8192))
                c.step("vector", lambda e, m=m, n2=n2, b=b: e.tensor_tensor(
                    xt[:, m, n2 * 512:(n2 + 1) * 512], xt[:, m, n2 * 512:(n2 + 1) * 512],
                    ps[:, b, :], ALU.add), reads=(bank(b),) + aregs(0, 8192),
                    writes=aregs(0, 8192))
        c.dma("sync", xor_[:, :, t0:t0 + 1024], xt[:, :, :], reads=aregs(0, 8192),
              writes=(r_xout,), sem=s["st"])



def build_program(block_names, dev_dump=False):
    nc = bass.Bass("TRN2", target_bir_lowering=False)
    env = {"R": {}, "dev_dump": dev_dump}

    cbf_off = {"cmean": 0, "ones_row": 1, "sel8": 129, "sel16": 129 + 8 * 128,
               "b64": 129 + 24 * 128, "idb": 129 + 25 * 128,
               "cmean256": 129 + 26 * 128, "mask": 129 + 26 * 128 + 1}
    CBF_W = 129 + 26 * 128 + 1 + 128
    cf32_off = {"id": 0, "eps_ln": 128, "eps_rms6": 129, "eps_fp32": 130,
                "bias2": 131}
    CF32_W = 132
    env["cbf_off"] = cbf_off
    env["cf32_off"] = cf32_off

    env["smask"] = nc.dram_tensor("smask", [128, 1], F32, kind="ExternalInput").ap()
    env["prev"] = nc.dram_tensor("prev", [1, 1], mybir.dt.int32, kind="ExternalInput").ap()
    cbf_h = nc.dram_tensor("cbf", [128, CBF_W], F32, kind="ExternalInput").ap()
    cf32_h = nc.dram_tensor("cf32", [128, CF32_W], F32, kind="ExternalInput").ap()

    chain = []
    x0 = nc.dram_tensor("x0", [D, T], F32, kind="ExternalInput").ap()
    xbufs = [x0]
    for i in range(len(block_names) - 1):
        xbufs.append(nc.dram_tensor(f"xi{i}", [D, T], F32, kind="Internal").ap())
    xout = nc.dram_tensor("xout", [D, T], F32, kind="ExternalOutput").ap()
    xbufs.append(xout)

    for bi, bn in enumerate(block_names):
        if bn == "l0f":
            env["l0f_x_in"] = xbufs[bi]
            env["l0f_x_out"] = xbufs[bi + 1]
            env["l0f_h1"] = nc.dram_tensor("l0f_h1s", [D, T], BF16,
                kind="ExternalOutput" if dev_dump else "Internal").ap()
            if dev_dump:
                env["l0f_cdump"] = nc.dram_tensor("l0f_cdump", [8, T], BF16,
                                                  kind="ExternalOutput").ap()
                env["l0f_gdump"] = nc.dram_tensor("l0f_gdump", [128, 1536], F32,
                                                  kind="ExternalOutput").ap()
            env["l0f_wL"] = nc.dram_tensor("l0f_wL", [128, 49152], F32, kind="ExternalInput").ap()
            env["l0f_wgate"] = nc.dram_tensor("l0f_wgate", [128, 64], F32, kind="ExternalInput").ap()
            env["l0f_wexp"] = nc.dram_tensor("l0f_wexp", [20, 3, 128, 4096], F32, kind="ExternalInput").ap()
            env["l0f_bias"] = nc.dram_tensor("l0f_bias", [128, 48], F32, kind="ExternalInput").ap()
        elif bn == "l0a":
            env["l0a_x_in"] = xbufs[bi]
            env["l0a_x_out"] = xbufs[bi + 1]
            env["l0a_wip"] = nc.dram_tensor("l0a_wip", [128, 40960], F32, kind="ExternalInput").ap()
            env["l0a_wop"] = nc.dram_tensor("l0a_wop", [128, 8192], F32, kind="ExternalInput").ap()
            env["l0a_wfc"] = nc.dram_tensor("l0a_wfc", [128, 32768], F32, kind="ExternalInput").ap()
            env["l0a_wpj"] = nc.dram_tensor("l0a_wpj", [128, 32768], F32, kind="ExternalInput").ap()
            env["l0a_n1"] = nc.dram_tensor("l0a_n1", [128, 8], F32, kind="ExternalInput").ap()
            env["l0a_n2"] = nc.dram_tensor("l0a_n2", [128, 8], F32, kind="ExternalInput").ap()
            for nm, dt_ in [("h1", BF16), ("qn", BF16), ("u", BF16), ("gate", BF16),
                            ("v", BF16), ("a", F32), ("Y", F32), ("C", BF16), ("t1", F32)]:
                env[f"l0a_{nm}"] = nc.dram_tensor(f"l0a_{nm}s", [D, T], dt_, kind="Internal").ap()
            env["l0a_sfc"] = nc.dram_tensor("l0a_sfcs", [4096, T], BF16, kind="Internal").ap()
            env["l0a_scontrib"] = nc.dram_tensor("l0a_scontrib", [1, 1024], F32, kind="Internal").ap()
            env["l0a_sgather"] = nc.dram_tensor("l0a_sgather", [8, 1024], F32, kind="Internal").ap()
        elif bn == "l1a":
            env["l1a_x_in"] = xbufs[bi]
            env["l1a_x_out"] = xbufs[bi + 1]
            TEx = T + 512
            env["l1a_wqkvb"] = nc.dram_tensor("l1a_wqkvb", [128, 3 * 8192 + 32], F32, kind="ExternalInput").ap()
            env["l1a_wconv"] = nc.dram_tensor("l1a_wconv", [128, 96], F32, kind="ExternalInput").ap()
            env["l1a_wfir"] = nc.dram_tensor("l1a_wfir", [128, 272], F32, kind="ExternalInput").ap()
            env["l1a_wfu1"] = nc.dram_tensor("l1a_wfu1", [4, 128, 16384], F32, kind="ExternalInput").ap()
            env["l1a_wfu1b"] = nc.dram_tensor("l1a_wfu1b", [128, 16], F32, kind="ExternalInput").ap()
            env["l1a_wfu2"] = nc.dram_tensor("l1a_wfu2", [128, 256], F32, kind="ExternalInput").ap()
            env["l1a_bfu2"] = nc.dram_tensor("l1a_bfu2", [128, 1], F32, kind="ExternalInput").ap()
            env["l1a_wo"] = nc.dram_tensor("l1a_wo", [128, 8192], F32, kind="ExternalInput").ap()
            env["l1a_wnorm"] = nc.dram_tensor("l1a_wnorm", [128, 3], F32, kind="ExternalInput").ap()
            env["l1a_wtau"] = nc.dram_tensor("l1a_wtau", [1, 4], F32, kind="ExternalInput").ap()
            env["l1a_pairlo"] = nc.dram_tensor("l1a_pairlo", [1, 1], mybir.dt.int32, kind="ExternalInput").ap()
            env["l1a_halfsel"] = nc.dram_tensor("l1a_halfsel", [1, 1], mybir.dt.int32, kind="ExternalInput").ap()
            for nm, dt_, sh in [("ln", BF16, [D, TEx]), ("zq", BF16, [D, TEx]),
                                ("zk", BF16, [D, TEx]), ("zv", BF16, [D, TEx]),
                                ("qn", BF16, [D, TEx]), ("kn", BF16, [D, TEx]),
                                ("kb", BF16, [D, TEx]), ("vb", BF16, [D, TEx]),
                                ("v", BF16, [D, TEx]), ("fs", BF16, [D, T]),
                                ("fl", BF16, [D, T]), ("wfe", BF16, [D, T]),
                                ("utok", BF16, [T, 2048]),
                                ("fstg2", BF16, [2, 512, 3 * T]),
                                ("tstg2", BF16, [2, T, 1024]),
                                ("ocon", BF16, [512, 2 * T]),
                                ("ogat", BF16, [1024, 2 * T])]:
                env[f"l1a_{nm}"] = nc.dram_tensor(f"l1a_{nm}s", sh, dt_, kind="Internal").ap()
            env["l1a_hcontrib"] = nc.dram_tensor("l1a_hcon", [1024, 512], F32, kind="Internal").ap()
            env["l1a_hgather"] = nc.dram_tensor("l1a_hgat", [2048, 512], F32, kind="Internal").ap()
            env["l1a_fcontrib"] = nc.dram_tensor("l1a_fcon", [1024, 3 * T], BF16, kind="Internal").ap()
            env["l1a_fgather"] = nc.dram_tensor("l1a_fgat", [2048, 3 * T], BF16, kind="Internal").ap()
            env["l1a_tcontrib"] = nc.dram_tensor("l1a_tcon", [T, 2048], BF16, kind="Internal").ap()
            env["l1a_tgather"] = nc.dram_tensor("l1a_tgat", [2 * T, 2048], BF16, kind="Internal").ap()
            env["l1a_ostg"] = nc.dram_tensor("l1a_ostg", [D, T], BF16, kind="Internal").ap()
        elif bn == "l1f":
            env["l1f_x_in"] = xbufs[bi]
            env["l1f_x_out"] = xbufs[bi + 1]
            env["l1f_ln"] = nc.dram_tensor("l1f_lns", [D, T], BF16, kind="Internal").ap()
            env["l1f_wgate"] = nc.dram_tensor("l1f_wgate", [128, 64], F32, kind="ExternalInput").ap()
            env["l1f_gbias"] = nc.dram_tensor("l1f_gbias", [128, 1], F32, kind="ExternalInput").ap()
            env["l1f_wexp"] = nc.dram_tensor("l1f_wexp", [64, 3, 128, 4096], F32, kind="ExternalInput").ap()
        else:
            raise ValueError(bn)

    emitters = {"l0f": emit_l0f, "l1f": emit_l1f, "l0a": emit_l0a, "l1a": emit_l1a}

    ps_h = nc.alloc_psum_tensor("ps", [128, 8, 512], F32)
    A32_h = nc.alloc_sbuf_tensor("A32", [128, 22528], F32)
    AB_h = nc.alloc_sbuf_tensor("AB", [128, 53248], BF16)
    SM_h = nc.alloc_sbuf_tensor("SM32", [128, 1600], F32)
    SMB_h = nc.alloc_sbuf_tensor("SMB", [128, 1024], BF16)
    CBF_h = nc.alloc_sbuf_tensor("CBF", [128, 129 + 26 * 128 + 1 + 128], BF16)
    CF32_h = nc.alloc_sbuf_tensor("CF32", [128, 132], F32)
    META_h = nc.alloc_sbuf_tensor("META", [1, 8], mybir.dt.int32)
    env["pt_i32"] = META_h.ap()[0:1, 0:1]
    env["pl_i32"] = META_h.ap()[0:1, 1:2]
    env["hs_i32"] = META_h.ap()[0:1, 2:3]
    env["tau_sb"] = META_h.ap()[0:1, 4:8].bitcast(F32)

    env["ps"] = ps_h.ap()
    env["A32"] = A32_h.ap()
    env["AB"] = AB_h.ap()
    env["SM32"] = SM_h.ap()
    env["SMB"] = SMB_h.ap()
    env["cbf"] = CBF_h.ap()
    env["cf32"] = CF32_h.ap()
    env["r_consts"] = Res()
    env["r_consts2"] = Res()

    sem_handles = {}
    sems = {}
    names_engs = [("ld", None), ("w", None), ("st", None), ("coll", None),
                  ("mm", "tensor"), ("dve", "vector"), ("act", "scalar"),
                  ("pool", "gpsimd")]

    from contextlib import ExitStack
    stack = ExitStack()
    for nm, eg in names_engs:
        h = stack.enter_context(nc.semaphore(nm))
        sem_handles[nm] = h

    def make_sems():
        return {nm: Sem(sem_handles[nm], nm, eg) for nm, eg in names_engs}

    def run_pass(which, eng):
        sems_p = make_sems()
        c = Ctx(nc, which, eng, sems_p, env)
        env["R"].clear()
        env["r_consts"] = Res()
        env["r_consts2"] = Res()
        c.dma("gpsimd", env["cbf"], cbf_h[:, :], writes=(env["r_consts"],))
        c.dma("sync", env["cf32"], cf32_h[:, :], writes=(env["r_consts2"],))
        for egn in ENGS:
            c.wait(egn, sems_p["w"], sems_p["w"].n)
            c.wait(egn, sems_p["ld"], sems_p["ld"].n)
        for bi_, bn in enumerate(block_names):
            if bi_ > 0:
                c.flush_waits()
                for egname in ENGS:
                    for snm in ("mm", "dve", "act", "pool", "ld", "w", "st", "coll"):
                        c.wait(egname, sems_p[snm], sems_p[snm].n)
                env["R"].clear()
                c.bank_prod = {}
                c.bank_cons = {}
            emitters[bn](c)
        c.flush_waits()
        final_st = sems_p["st"].n
        final_w = sems_p["w"].n
        final_ld = sems_p["ld"].n
        if which == "gpsimd":
            eng.wait_ge(sem_handles["st"], final_st)
            eng.wait_ge(sem_handles["w"], final_w)
            eng.wait_ge(sem_handles["ld"], final_ld)
        return sems_p

    with stack:
        with nc.Block() as blk:
            @blk.sync
            def _(eng):
                run_pass("sync", eng)

            @blk.gpsimd
            def _(eng):
                run_pass("gpsimd", eng)

            @blk.tensor
            def _(eng):
                run_pass("tensor", eng)

            @blk.vector
            def _(eng):
                run_pass("vector", eng)

            @blk.scalar
            def _(eng):
                run_pass("scalar", eng)

    return nc



def host_consts():
    W = 129 + 26 * 128 + 1 + 128
    cbf = np.zeros((128, W), np.float32)
    cbf[:, 0] = 1.0 / 1024.0
    cbf[:, 1:129] = 1.0
    for e in range(8):
        cbf[e, 129 + e * 128: 129 + (e + 1) * 128] = 1.0
    s16 = 129 + 8 * 128
    for e in range(16):
        cbf[e, s16 + e * 128: s16 + (e + 1) * 128] = 1.0
    cf32 = np.zeros((128, 132), np.float32)
    cf32[:, :128] = np.eye(128, dtype=np.float32)
    cf32[:, 128] = 1e-5
    cf32[:, 129] = 1e-6
    cf32[:, 130] = float(np.finfo(np.float32).eps)
    cf32[:, 131] = 2.0
    b64 = np.zeros((128, 128), np.float32)
    b64[:64, :64] = 1.0 / 64.0
    b64[64:, 64:] = 1.0 / 64.0
    cbf[:, 129 + 24 * 128: 129 + 25 * 128] = b64
    cbf[:, 129 + 25 * 128: 129 + 26 * 128] = np.eye(128, dtype=np.float32)
    cbf[:, 129 + 26 * 128] = 1.0 / 256.0
    mo = 129 + 26 * 128 + 1
    sl = np.tril(np.ones((32, 32), np.float32), -1)
    su = np.triu(np.ones((32, 32), np.float32), 1)
    ey = np.eye(32, dtype=np.float32)
    ui = np.triu(np.ones((32, 32), np.float32))
    for rr in range(4):
        cbf[rr * 32:(rr + 1) * 32, mo:mo + 32] = sl
        cbf[rr * 32:(rr + 1) * 32, mo + 32:mo + 64] = su
        cbf[rr * 32:(rr + 1) * 32, mo + 64:mo + 96] = ey
        cbf[rr * 32:(rr + 1) * 32, mo + 96:mo + 128] = ui
    return cbf, cf32


def prep_l0f(inputs):
    wL = np.concatenate([
        lhst(inputs["L0_ffn_lin0_w"]), lhst(inputs["L0_ffn_swi1_w1"]),
        lhst(inputs["L0_ffn_swi1_w2"]),
        lhst(inputs["L0_ffn_lin1_w"]), lhst(inputs["L0_ffn_swi2_w1"]),
        lhst(inputs["L0_ffn_swi2_w2"])], axis=1)
    wgate = lhst_small(np.ascontiguousarray(inputs["L0_ffn_gate_w"].T))
    exp = []

    def add_pe(w1b, w3b, w2b):
        for hh in range(2):
            hs = slice(hh * 512, (hh + 1) * 512)
            exp.append(np.stack([lhst(np.ascontiguousarray(w1b[:, hs])),
                                 lhst(np.ascontiguousarray(w3b[:, hs])),
                                 lhst(np.ascontiguousarray(w2b[hs, :]))]))
    for e in range(8):
        add_pe(inputs["L0_ffn_exp_w1"][e], inputs["L0_ffn_exp_w3"][e],
               inputs["L0_ffn_exp_w2"][e])
    sh1, sh3, sh2 = inputs["L0_ffn_sh_w1"], inputs["L0_ffn_sh_w3"], inputs["L0_ffn_sh_w2"]
    add_pe(sh1[:, :1024], sh3[:, :1024], sh2[:1024, :])
    add_pe(sh1[:, 1024:], sh3[:, 1024:], sh2[1024:, :])
    wexp = np.stack(exp)
    bias_cols = np.concatenate([
        col128(inputs["L0_ffn_lin0_b"]), col128(inputs["L0_ffn_swi1_b1"]),
        col128(inputs["L0_ffn_swi1_b2"]), col128(inputs["L0_ffn_lin1_b"]),
        col128(inputs["L0_ffn_swi2_b1"]), col128(inputs["L0_ffn_swi2_b2"])], axis=1)
    return {"l0f_wL": wL, "l0f_wgate": wgate, "l0f_wexp": wexp, "l0f_bias": bias_cols}


def prep_l0a(inputs):
    return {
        "l0a_wip": np.concatenate([
            lhst(np.ascontiguousarray(inputs["L0_attn_in_proj_w"].T[:, 0:1024])),
            lhst(np.ascontiguousarray(inputs["L0_attn_in_proj_w"].T[:, 1024:3072])),
            lhst(np.ascontiguousarray(inputs["L0_attn_in_proj_w"].T[:, 3072:5120]))], axis=1),
        "l0a_wop": lhst(np.ascontiguousarray(inputs["L0_attn_out_proj_w"].T)),
        "l0a_wfc": lhst(np.ascontiguousarray(inputs["L0_attn_fc_w"].T)),
        "l0a_wpj": lhst(np.ascontiguousarray(inputs["L0_attn_proj_w"].T)),
        "l0a_n1": col128(inputs["L0_attn_norm1_w"]),
        "l0a_n2": col128(inputs["L0_attn_norm2_w"]),
    }


def prep_l1a(inputs):
    ipq = lhst(np.ascontiguousarray(inputs["L1_attn_W_q"]))
    ipk = lhst(np.ascontiguousarray(inputs["L1_attn_W_k"]))
    ipv = lhst(np.ascontiguousarray(inputs["L1_attn_W_v"]))
    ipb = lhst_small(np.ascontiguousarray(inputs["L1_attn_W_b"]))
    wconv = np.zeros((128, 96), np.float32)
    for ti, nm in enumerate(["conv_w_q", "conv_w_k", "conv_w_v"]):
        w = inputs[f"L1_attn_{nm}"]
        for dc in range(8):
            for j in range(4):
                wconv[:, 17 - 17 + ti * 32 + dc * 4 + j] = w[dc * 128:(dc + 1) * 128, j]
    wfir = np.zeros((128, 272), np.float32)
    fs = inputs["L1_attn_fir_short_w"].reshape(1024, 3)
    fl = inputs["L1_attn_fir_long_w"].reshape(1024, 31)
    for dc in range(8):
        for j in range(3):
            wfir[:, dc * 34 + j] = fs[dc * 128:(dc + 1) * 128, j]
        for j in range(31):
            wfir[:, dc * 34 + 3 + j] = fl[dc * 128:(dc + 1) * 128, j]
    wf1 = inputs["L1_attn_W_fuse1"]
    wfu1 = np.stack([lhst(np.ascontiguousarray(wf1[q * 1024:(q + 1) * 1024, :]))
                     for q in range(4)])
    wfu1b = np.zeros((128, 16), np.float32)
    b1 = inputs["L1_attn_b_fuse1"]
    for m in range(16):
        wfu1b[:, m] = b1[m * 128:(m + 1) * 128]
    wfu2 = lhst_small(np.ascontiguousarray(inputs["L1_attn_W_fuse2"]))
    bfu2 = np.zeros((128, 1), np.float32)
    bfu2[0:16, 0] = inputs["L1_attn_b_fuse2"]
    wo = lhst(np.ascontiguousarray(inputs["L1_attn_W_o"]))
    wnorm = np.zeros((128, 3), np.float32)
    nw = inputs["L1_attn_norm_w"]
    wnorm[:, 0] = nw[0:128]
    wnorm[:, 1] = nw[128:256]
    wtau = inputs["L1_attn_others_log_tau"].reshape(1, 4)
    return {"l1a_wqkvb": np.concatenate([ipq, ipk, ipv, ipb], axis=1),
            "l1a_wconv": wconv, "l1a_wfir": wfir, "l1a_wfu1": wfu1,
            "l1a_wfu1b": wfu1b, "l1a_wfu2": wfu2, "l1a_bfu2": bfu2,
            "l1a_wo": wo, "l1a_wnorm": wnorm, "l1a_wtau": wtau}


def prep_l1f(inputs):
    wgate = lhst_small(np.ascontiguousarray(inputs["L1_ffn_gate_w"]))
    gb = np.zeros((128, 1), np.float32)
    gb[0:8, 0] = inputs["L1_ffn_gate_b"]
    w1, w3, w2 = inputs["L1_ffn_w1"], inputs["L1_ffn_w3"], inputs["L1_ffn_w2"]
    pes = []
    for e in range(8):
        for blk in range(4):
            sl = slice(blk * 1024, (blk + 1) * 1024)
            w1b, w3b, w2b = w1[e][:, sl], w3[e][:, sl], w2[e][sl, :]
            for hh in range(2):
                hs = slice(hh * 512, (hh + 1) * 512)
                pes.append(np.stack([lhst(np.ascontiguousarray(w1b[:, hs])),
                                     lhst(np.ascontiguousarray(w3b[:, hs])),
                                     lhst(np.ascontiguousarray(w2b[hs, :]))]))
    return {"l1f_wgate": wgate, "l1f_gbias": gb, "l1f_wexp": np.stack(pes)}


def shard_x(x):
    shards = []
    for cidx in range(NCORES):
        b, h = cidx // 2, cidx % 2
        shards.append(np.ascontiguousarray(x[b, h * T:(h + 1) * T, :].T))
    return shards


def unshard_x(shards):
    x = np.empty((4, 8192, D), np.float32)
    for cidx in range(NCORES):
        b, h = cidx // 2, cidx % 2
        x[b, h * T:(h + 1) * T, :] = shards[cidx].T
    return x


def run_blocks(block_names, x_full, inputs):
    cbf, cf32 = host_consts()
    nc = build_program(block_names)
    weight_maps = {}
    for bn in block_names:
        if bn == "l0f":
            weight_maps.update(prep_l0f(inputs))
        elif bn == "l1f":
            weight_maps.update(prep_l1f(inputs))
        elif bn == "l0a":
            weight_maps.update(prep_l0a(inputs))
        elif bn == "l1a":
            weight_maps.update(prep_l1a(inputs))
    shards = shard_x(x_full)
    in_maps = []
    for cidx in range(NCORES):
        m = {"x0": shards[cidx], "cbf": cbf, "cf32": cf32,
             "smask": np.full((128, 1), float(cidx % 2), np.float32),
             "prev": np.array([[(cidx - 1) % 8]], np.int32),
             "l1a_pairlo": np.array([[2 * (cidx // 2)]], np.int32),
             "l1a_halfsel": np.array([[cidx % 2]], np.int32)}
        m.update(weight_maps)
        in_maps.append(m)
    res = run_bass_kernel_spmd(nc, in_maps, core_ids=list(range(NCORES)))
    return unshard_x([res.results[i]["xout"] for i in range(NCORES)])


def kernel(**inputs):
    x = np.asarray(inputs["L0_attn_x"], np.float32)
    return run_blocks(["l0a", "l0f", "l1a", "l1f"], x, inputs)
